# version 03 (final)
# baseline (speedup 1.0000x reference)
import contextlib
import numpy as np
import ml_dtypes
import concourse.bass as bass
import concourse.mybir as mybir
from concourse.bass_utils import run_bass_kernel_spmd

F32 = mybir.dt.float32
BF16 = mybir.dt.bfloat16
AF = mybir.ActivationFunctionType
ALU = mybir.AluOpType
AX = mybir.AxisListType

NCORES = 8
T = 2048
HALO = 4
TH = T + HALO
DM = 1024
DCH = DM // 128
SEQ = 8192
EPS_LN = 1e-5

ENGINES = ("sync", "gpsimd", "tensor", "vector", "scalar")


class KB:

    def __init__(self, nc, stk):
        self.nc = nc
        self.stk = stk
        self.q = {e: [] for e in ENGINES}
        self.semh = {}
        self.plan = {}
        self.bufs = {}
        self.dram = {}

    def sem(self, name):
        if name not in self.semh:
            self.semh[name] = self.stk.enter_context(self.nc.semaphore(name))
            self.plan[name] = 0
        return self.semh[name]

    def sbuf(self, name, shape, dt):
        if name not in self.bufs:
            stk = getattr(self, 'cur_scope', None) or self.stk
            self._uniq = getattr(self, '_uniq', 0) + 1
            pname = name if not getattr(self, 'fused', False) else '%s_g%d' % (name, self._uniq)
            self.bufs[name] = stk.enter_context(self.nc.sbuf_tensor(pname, shape, dt))
        return self.bufs[name]

    def dram_scratch(self, name, shape, dt):
        self._uniq = getattr(self, '_uniq', 0) + 1
        pname = name if not getattr(self, 'fused', False) else '%s_g%d' % (name, self._uniq)
        return self.nc.dram_tensor(pname, shape, dt).ap()

    def scope(self):
        import contextlib as _cl

        class _Scope:
            def __init__(sf, kb):
                sf.kb = kb

            def __enter__(sf):
                sf.stk = _cl.ExitStack()
                sf.names = set(sf.kb.bufs)
                sf.prev = getattr(sf.kb, 'cur_scope', None)
                sf.kb.cur_scope = sf.stk
                return sf

            def __exit__(sf, *a):
                sf.kb.cur_scope = sf.prev
                for n in list(sf.kb.bufs):
                    if n not in sf.names:
                        del sf.kb.bufs[n]
                sf.stk.close()
                return False
        return _Scope(self)

    def fence_all(self):
        waits = [(n, v) for n, v in self.plan.items() if v > 0]
        for e in ('tensor', 'vector', 'scalar', 'gpsimd', 'sync'):
            self.barrier(e, waits)

    def psum(self, name, shape, dt=F32):
        if name not in self.bufs:
            self.bufs[name] = self.stk.enter_context(self.nc.psum_tensor(name, shape, dt))
        return self.bufs[name]

    def op(self, eng, fn):
        self.q[eng].append(fn)

    def inc(self, name, n=1):
        self.sem(name)
        self.plan[name] += n
        return self.plan[name]

    def val(self, name):
        self.sem(name)
        return self.plan[name]

    def step(self, eng, fn, waits=(), inc=None, inc_n=1, dep=None):
        waits = list(waits)
        if dep is not None:
            for dn in ([dep] if isinstance(dep, str) else dep):
                waits.append((dn, self.val(dn)))
        waits = [(n, v) for (n, v) in waits if v > 0]
        sems = {n: self.sem(n) for n, _ in waits}
        ih = self.sem(inc) if inc is not None else None
        if inc is not None:
            self.plan[inc] += inc_n

        def run(e, fn=fn, waits=tuple(waits)):
            for n, v in waits:
                e.wait_ge(sems[n], v)
            r = fn(e)
            if ih is not None:
                r.then_inc(ih, inc_n)
        self.op(eng, run)
        return self.plan[inc] if inc is not None else None

    def dma(self, eng, out_ap, in_ap, waits=(), inc='ld'):
        return self.step(eng, lambda e: e.dma_start(out=out_ap, in_=in_ap),
                         waits=waits, inc=inc, inc_n=16)

    def barrier(self, eng, waits):
        self.step(eng, lambda e: e.nop(), waits=waits)

    def now(self, *names):
        return [(n, self.val(n)) for n in names]

    def finalize(self, block):
        for e in ENGINES:
            fns = self.q[e]
            if not fns:
                continue
            getattr(block, e)(lambda eng, fns=fns: [f(eng) for f in fns])


def tok_blocks(width=512, total=T, off0=HALO):
    return [(off0 + i, min(width, total - i)) for i in range(0, total, width)]



def emit_ln(kb, x, xln, stats, halo=True, blocks=None):
    if blocks is None:
        blocks = ([(0, HALO)] if halo else []) + tok_blocks()
    ones_col = kb.bufs['ones_col']
    ones_row = kb.bufs['ones_row']
    ones_row32 = kb.bufs.get('ones_row32', None)
    if ones_row32 is None:
        ones_row32 = kb.sbuf('ones_row32', [33, 128], F32)
        kb.step('vector', lambda v, o=ones_row32[0:33, :]: v.memset(o, 1.0), inc='dv')
    kb.bufs['ones_row32'] = ones_row32
    scr = kb.bufs['scr_f32']
    sqscr = scr[:, 0:1024]
    tscr = scr[:, 0:512]
    ps0, ps1 = kb.bufs['ps0'], kb.bufs['ps1']
    ps_sx = ps0[0:1, 0:512]
    ps_sq = ps0[0:1, 512:1024]
    ps_bc = ps1[:, 0:1024]
    kb.barrier('tensor', kb.now('dv', 'ac'))

    for off, w in blocks:
        for d in range(DCH):
            xv = x[:, d * TH + off: d * TH + off + w]
            sq = sqscr[:, (d % 2) * 512:(d % 2) * 512 + w]
            kb.step('scalar', lambda s, o=sq, i=xv: s.activation(o, i, AF.Square),
                    waits=kb.now('pe') + (kb.now('dv') if d == 0 else []), inc='ac')
            kb.step('tensor', lambda t, p=ps_sx[0:1, 0:w], r=xv, f=(d == 0), l=(d == DCH - 1):
                    t.matmul(p, ones_col[:, :], r, start=f, stop=l),
                    waits=kb.now('ac') + (kb.now('dv') if d == 0 else []))
            kb.step('tensor', lambda t, p=ps_sq[0:1, 0:w], r=sq, f=(d == 0), l=(d == DCH - 1):
                    t.matmul(p, ones_col[:, :], r, start=f, stop=l), inc='pe')
        pe_done = kb.val('pe')
        m = stats[0:1, off:off + w]
        rstd = stats[32:33, off:off + w]
        var = stats[64:65, off:off + w]
        t3 = scr[0:1, 0:w]
        t4 = scr[0:1, 512:512 + w]
        kb.step('vector', lambda v, o=m, i=ps_sx[0:1, 0:w]:
                v.tensor_scalar(o, i, 1.0 / DM, None, ALU.mult), waits=[('pe', pe_done)],
                inc='dv')
        kb.step('vector', lambda v, o=t3, i=ps_sq[0:1, 0:w]:
                v.tensor_scalar(o, i, 1.0 / DM, None, ALU.mult), inc='dv')
        kb.step('scalar', lambda s, o=t4, i=m: s.activation(o, i, AF.Square),
                waits=kb.now('dv'), inc='ac')
        kb.step('vector', lambda v, o=var, a=t3, b=t4: v.tensor_tensor(o, a, b, ALU.subtract),
                waits=kb.now('ac'), inc='dv')
        kb.step('vector', lambda v, o=t3, i=var: v.tensor_scalar(o, i, EPS_LN, None, ALU.add),
                dep='dv', inc='dv')
        kb.step('scalar', lambda s, o=t4, i=t3: s.activation(o, i, AF.Ln), waits=kb.now('dv'),
                inc='ac')
        kb.step('scalar', lambda s, o=rstd, i=t4: s.activation(o, i, AF.Exp, scale=-0.5),
                dep='ac', inc='ac')
        kb.step('tensor', lambda t, p=ps_bc[:, 0:w], r=m:
                t.matmul(p, ones_row[:, :], r, start=True, stop=True),
                waits=kb.now('ac', 'dv'))
        kb.step('tensor', lambda t, p=ps_bc[:, 512:512 + w], r=rstd:
                t.matmul(p, ones_row32[32:33, :], r, start=True, stop=True), inc='pe')
        pe_bc = kb.val('pe')
        for d in range(DCH):
            xv = x[:, d * TH + off: d * TH + off + w]
            ov = xln[:, d * TH + off: d * TH + off + w]
            kb.step('vector', lambda v, o=tscr[:, 0:w], a=xv, b=ps_bc[:, 0:w]:
                    v.tensor_tensor(o, a, b, ALU.subtract),
                    waits=[('pe', pe_bc)] if d == 0 else [], dep='dv', inc='dv')
            kb.step('vector', lambda v, o=ov, a=tscr[:, 0:w], b=ps_bc[:, 512:512 + w]:
                    v.tensor_tensor(o, a, b, ALU.mult), dep='dv', inc='dv')
        kb.barrier('tensor', kb.now('dv'))



def emit_proj(kb, tag, W_dram, in_dim, out_dim, in_view, epilogue,
              tblocks=None, wq='gpsimd', w_hook=None):
    kin = in_dim // 128
    kout = (out_dim + 127) // 128
    if tblocks is None:
        tblocks = tok_blocks()
    assert sum(w for _, w in tblocks) <= 2048 and kin <= 32
    wcols = kin * 128
    if getattr(kb, 'pw_single', False):
        b0 = kb.sbuf('pw0', [128, kb.pw_cols], BF16)
        wbuf = [b0, b0]
    else:
        wbuf = [kb.sbuf(f'pw{i}', [128, kb.pw_cols], BF16) for i in range(2)]
    assert wcols <= kb.pw_cols
    ps = [kb.bufs['ps0'], kb.bufs['ps1']]
    pes = 'pe_proj'
    pbase = kb.val(pes)
    kb.barrier('tensor', kb.now('dv', 'ac'))
    prev_proj = getattr(kb, 'last_proj_done', [])
    ep_marks = []
    single = getattr(kb, 'pw_single', False)
    NG = 4 if kin % 4 == 0 else 1
    gk = kin // NG
    total_w = sum(ww for _, ww in tblocks)
    for o in range(kout):
        ow = min(128, out_dim - o * 128)
        wb = wbuf[o % 2]
        pb = ps[o % 2]
        w3 = W_dram[:, o * 128:o * 128 + ow].rearrange("(kc p) o -> p kc o", p=128)
        gld = {}
        if single:
            for g in range(NG):
                lds = 'ld_pwg%d' % g
                kb.dma(wq, wb[:, g * gk * ow:(g + 1) * gk * ow]
                       .rearrange("p (kc o) -> p kc o", kc=gk), w3[:, g * gk:(g + 1) * gk, :],
                       waits=([(pes, pbase + (o - 1) * NG + g + 1)] if o >= 1 else prev_proj),
                       inc=lds)
                gld[g] = kb.val(lds)
        else:
            lds = 'ld_pw' + str(o % 2)
            kb.dma(wq, wb[:, 0:kin * ow].rearrange("p (kc o) -> p kc o", kc=kin), w3,
                   waits=([(pes, pbase + (o - 1) * NG)] if o >= 2 else prev_proj), inc=lds)
            for g in range(NG):
                gld[g] = (lds, kb.val(lds))
        cum = 0
        for off, w in tblocks:
            for k in range(kin):
                mw = []
                if cum == 0 and k % gk == 0:
                    g = k // gk
                    mw = [('ld_pwg%d' % g, gld[g])] if single else [gld[g]]
                if k == 0 and cum == 0 and o >= 2:
                    mw = mw + ep_marks[o - 2]
                kb.step('tensor',
                        lambda t, p=pb[0:ow, cum:cum + w], l=wb[:, k * ow:(k + 1) * ow],
                               r=in_view(k, off, w), f=(k == 0), lst=(k == kin - 1):
                        t.matmul(p, l, r, start=f, stop=lst),
                        waits=mw,
                        inc=pes if (cum + w >= total_w and (k + 1) % gk == 0) else None)
            cum += w
        pe_done = kb.val(pes)
        cum = 0
        for bi, (off, w) in enumerate(tblocks):
            epilogue(o, (off, w), pb[0:ow, cum:cum + w],
                     [(pes, pe_done)] if bi == 0 else [])
            cum += w
        ep_marks.append(kb.now('dv', 'ac'))
    kb.last_proj_done = [(pes, pbase + kout * NG)]



def emit_l1ffn(kb, x, xln, wr_key, wk_key, wv_key):
    mu = kb.bufs['mu_sb']
    HT = T // 2
    xr = kb.sbuf('rk_A', [128, DCH * HT], BF16)
    xk = kb.sbuf('rk_B', [128, DCH * HT], BF16)
    rr = kb.sbuf('rk_C', [128, DCH * HT], BF16)
    ka = xr
    relu_scr = kb.bufs['scr_f32'][:, 0:512]
    om = mu[:, 16:32]
    kb.step('vector', lambda v, o=om, i=mu[:, 0:16]:
            v.tensor_scalar(o, i, -1.0, 1.0, ALU.mult, ALU.add), inc='dv')

    for half in range(2):
        t0 = half * HT
        tblocks = [(HALO + t0 + j * 512, 512) for j in range(HT // 512)]
        for d in range(DCH):
            cur = xln[:, d * TH + HALO + t0: d * TH + HALO + t0 + HT]
            prv = xln[:, d * TH + HALO + t0 - 1: d * TH + HALO + t0 - 1 + HT]
            sl = slice(d * HT, (d + 1) * HT)
            kb.step('vector', lambda v, o=xr[:, sl], i=cur, m=mu[:, d:d + 1]:
                    v.tensor_scalar(o, i, m, None, ALU.mult),
                    waits=kb.now('ac') if d == 0 else [], inc='dv')
            kb.step('vector', lambda v, o=xr[:, sl], i=prv, m=om[:, d:d + 1], p=xr[:, sl]:
                    v.scalar_tensor_tensor(o, i, m, p, ALU.mult, ALU.add), dep='dv', inc='dv')
            kb.step('vector', lambda v, o=xk[:, sl], i=cur, m=mu[:, 8 + d:8 + d + 1]:
                    v.tensor_scalar(o, i, m, None, ALU.mult), inc='dv')
            kb.step('vector', lambda v, o=xk[:, sl], i=prv, m=om[:, 8 + d:8 + d + 1], p=xk[:, sl]:
                    v.scalar_tensor_tensor(o, i, m, p, ALU.mult, ALU.add), dep='dv', inc='dv')
        prep = kb.val('dv')

        def iv(buf, t0=t0):
            return lambda k, off, w: buf[:, k * HT + off - HALO - t0: k * HT + off - HALO - t0 + w]

        def ep_r(o, blk, ps_ap, waits, t0=t0):
            off, w = blk
            kb.step('scalar', lambda s, o_=rr[:, o * HT + off - HALO - t0: o * HT + off - HALO - t0 + w],
                           i=ps_ap:
                    s.activation(o_, i, AF.Sigmoid), waits=waits + [('dv', prep)], inc='ac')
        ep_r.engine = 'scalar'
        emit_proj(kb, f'l1f_r{half}', kb.dram[wr_key], DM, DM, iv(xr), ep_r, tblocks=tblocks)

        def ep_k(o, blk, ps_ap, waits, t0=t0):
            off, w = blk
            kb.step('vector', lambda v, o_=relu_scr[:, 0:w], i=ps_ap:
                    v.tensor_scalar(o_, i, 0.0, None, ALU.max), waits=waits + kb.now('ac'),
                    inc='dv')
            kb.step('scalar', lambda s, o_=ka[:, o * HT + off - HALO - t0: o * HT + off - HALO - t0 + w],
                           i=relu_scr[:, 0:w]:
                    s.activation(o_, i, AF.Square), waits=kb.now('dv'), inc='ac')
        ep_k.engine = 'scalar'
        emit_proj(kb, f'l1f_k{half}', kb.dram[wk_key], DM, DM, iv(xk), ep_k, tblocks=tblocks)

        def ep_v(o, blk, ps_ap, waits, t0=t0):
            off, w = blk
            kb.step('vector', lambda v, o_=relu_scr[:, 0:w], a=ps_ap,
                           b=rr[:, o * HT + off - HALO - t0: o * HT + off - HALO - t0 + w]:
                    v.tensor_tensor(o_, a, b, ALU.mult), waits=waits + kb.now('ac'),
                    dep='dv', inc='dv')
            kb.step('vector', lambda v, o_=x[:, o * TH + off: o * TH + off + w],
                           a=x[:, o * TH + off: o * TH + off + w], b=relu_scr[:, 0:w]:
                    v.tensor_tensor(o_, a, b, ALU.add), dep='dv', inc='dv')
        ep_v.engine = 'vector'
        emit_proj(kb, f'l1f_v{half}', kb.dram[wv_key], DM, DM, iv(ka), ep_v, tblocks=tblocks)
        kb.barrier('vector', kb.now('pe_proj'))
        kb.barrier('scalar', kb.now('pe_proj'))



def emit_ffn0(kb, x, xln, w1, w2, h1, h2, w3):
    sc = kb.bufs['fsc_sb']
    g1 = kb.sbuf('ffn_g1', [128, 32 * 512], BF16)
    hh = kb.sbuf('ffn_h', [128, 32 * 512], BF16)
    DH = 4096

    for ti in range(4):
        t0 = HALO + ti * 512
        tb = [(t0, 512)]
        xv = lambda k, off, w: xln[:, k * TH + off: k * TH + off + w]
        g1v = lambda k, off, w: g1[:, k * 512: k * 512 + w]
        hhv = lambda k, off, w: hh[:, k * 512: k * 512 + w]

        def ep1(o, blk, ps_ap, waits):
            kb.step('scalar', lambda s_, o_=g1[:, o * 512:(o + 1) * 512], i=ps_ap,
                           m=sc[:, o:o + 1]:
                    s_.activation(o_, i, AF.Silu, scale=m), waits=waits, inc='ac')
        ep1.engine = 'scalar'
        emit_proj(kb, f'f1_{ti}', kb.dram[w1], DM, DH, xv, ep1, tblocks=tb)

        def ep2(o, blk, ps_ap, waits):
            kb.step('vector', lambda v, o_=hh[:, o * 512:(o + 1) * 512], i=ps_ap,
                           m=sc[:, 32 + o:32 + o + 1], g=g1[:, o * 512:(o + 1) * 512]:
                    v.scalar_tensor_tensor(o_, i, m, g, ALU.mult, ALU.mult),
                    waits=waits + kb.now('ac'), inc='dv')
        ep2.engine = 'vector'
        emit_proj(kb, f'f2_{ti}', kb.dram[w2], DM, DH, xv, ep2, tblocks=tb)

        def ep3(o, blk, ps_ap, waits):
            kb.step('scalar', lambda s_, o_=g1[:, o * 512:(o + 1) * 512], i=ps_ap,
                           m=sc[:, 64 + o:64 + o + 1]:
                    s_.activation(o_, i, AF.Gelu, scale=m), waits=waits, inc='ac')
        ep3.engine = 'scalar'
        emit_proj(kb, f'f3_{ti}', kb.dram[h1], DH, DH, hhv, ep3, tblocks=tb)

        def ep4(o, blk, ps_ap, waits):
            kb.step('scalar', lambda s_, o_=hh[:, o * 512:(o + 1) * 512], i=ps_ap,
                           m=sc[:, 96 + o:96 + o + 1]:
                    s_.activation(o_, i, AF.Gelu, scale=m), waits=waits, inc='ac')
        ep4.engine = 'scalar'
        emit_proj(kb, f'f4_{ti}', kb.dram[h2], DH, DH, g1v, ep4, tblocks=tb)

        def ep5(o, blk, ps_ap, waits):
            off, w = blk
            kb.step('vector', lambda v, o_=x[:, o * TH + off: o * TH + off + w], i=ps_ap,
                           m=sc[:, 128 + o:128 + o + 1], xx=x[:, o * TH + off: o * TH + off + w]:
                    v.scalar_tensor_tensor(o_, i, m, xx, ALU.mult, ALU.add),
                    waits=waits + kb.now('ac'), inc='dv')
        ep5.engine = 'vector'
        emit_proj(kb, f'f5_{ti}', kb.dram[w3], DH, DM, hhv, ep5, tblocks=tb)



KTILES = SEQ // 128
RND = 2
NR = KTILES // RND

def emit_flash_st(kb, tag, nheads, dq, dv, qap, kld, vld, expbias, expscale,
                  out_cb, bias_mms=None, vshare=1, fin_base=lambda h: 0):
    nc = kb.nc
    ps0, ps1 = kb.bufs['ps0'], kb.bufs['ps1']
    sps = [ps0[:, 0:1024], ps1[:, 0:1024]]
    avs = [ps0[:, 1024:1536], ps1[:, 1024:1536]]
    fin = [ps0[:, 1536:2048], ps1[:, 1536:2048]]
    kb.flash_fin = fin
    ksb = [kb.sbuf(f'fl_k{i}', [2 * dq, KTILES * 128], BF16) for i in range(2)]
    vsb = [kb.sbuf(f'fl_v{i}', [128, KTILES * (dv + 1)], BF16) for i in range(2)]
    psb = [kb.sbuf(f'fl_p{i}', [128, RND * 512], BF16) for i in range(2)]
    fzb = kb.bufs['scr_f32'][0:1, 512:1024]
    finsb = kb.sbuf('fl_fin', [128, 512], F32)
    kb.barrier('tensor', kb.now('dv', 'ac'))

    for i in range(2):
        kb.step('vector', lambda v, o=vsb[i].ap().rearrange(
                    "p (n d) -> p n d", d=dv + 1)[:, :, dv:dv + 1]:
                v.memset(o, 1.0), inc='dv')
    vones = kb.val('dv')

    qk_mark = {}
    av_free = {}
    for h in range(nheads):
        kofs = (h % 2) * dq
        kbuf = ksb[h % 2][kofs:kofs + dq, :]
        kld(h, kbuf, ([('pe', qk_mark[h - 2])] if h >= 2 else []), 'ldk%d' % (h % 2))
        ldk = kb.val('ldk%d' % (h % 2))
        if h % vshare == 0:
            vh = h // vshare
            vbuf = vsb[vh % 2]
            vw = ([('pe', qk_mark[h - vshare - 1])] if vh >= 2 else [('dv', vones)])
            vld(vh, vbuf, vw, 'ldv%d' % (vh % 2))
        vbuf = vsb[(h // vshare) % 2]
        ldv = kb.val('ldv%d' % ((h // vshare) % 2))

        for qb in range(4):
            q_ap = qap(h, qb)
            ac_marks = {}
            for r in range(NR + 1):
                sp = sps[r % 2]
                if r < NR:
                    mm = []
                    for i in range(RND):
                        kt = r * RND + i
                        w0 = []
                        if i == 0:
                            w0 = [('ac', ac_marks[r - 2])] if r >= 2 else []
                            if r == 0:
                                w0 += [('ldk%d' % (h % 2), ldk),
                                       ('ldv%d' % ((h // vshare) % 2), ldv)]
                        bfns = (bias_mms(h, qb, kt, sp[:, i * 512:(i + 1) * 512])
                                if bias_mms is not None else [])
                        mm.append((lambda t, p=sp[:, i * 512:(i + 1) * 512],
                                          l=kbuf[:, kt * 128:(kt + 1) * 128], r_=q_ap:
                                   t.matmul(p, l, r_, start=True, stop=True), w0))
                        for bfn in bfns:
                            mm.append((bfn, []))
                    for j, (fn, w0) in enumerate(mm):
                        kb.step('tensor', fn, waits=w0,
                                inc='pe' if j == len(mm) - 1 else None)
                    pe_qk = kb.val('pe')
                if r >= 1:
                    rp = r - 1
                    pbp = psb[rp % 2]
                    for i in range(RND):
                        kt = rp * RND + i
                        w1 = []
                        if i == 0:
                            w1 = [('ac', ac_marks[rp])]
                            if rp == 0 and (h, qb) not in ((0, 0), (0, 1)):
                                key = (h, qb - 2) if qb >= 2 else (h - 1, qb + 2)
                                if key in av_free:
                                    w1.append(('dv', av_free[key]))
                        kb.step('tensor', lambda t, p=avs[qb % 2][0:dv + 1, :],
                                       l=vbuf[:, kt * (dv + 1):(kt + 1) * (dv + 1)],
                                       r_=pbp[:, i * 512:(i + 1) * 512], f=(kt == 0):
                                t.matmul(p, l, r_, start=f, stop=(kt == KTILES - 1),
                                         skip_group_check=True),
                                waits=w1, inc='pe' if kt == KTILES - 1 else None)
                if r < NR:
                    pb = psb[r % 2]
                    kb.step('scalar', lambda sc, o=pb[:, 0:RND * 512], i=sp[:, 0:RND * 512],
                                   b=expbias(h), m=expscale(h):
                            sc.activation(o, i, AF.Exp, bias=b, scale=m),
                            waits=[('pe', pe_qk)], inc='ac')
                    ac_marks[r] = kb.val('ac')
            av = avs[qb % 2]
            assert dv == 64
            fb = fin_base(h)
            kb.step('vector', lambda v, o=fzb, i=av[64:65, 0:512]:
                    v.reciprocal(o, i), waits=kb.now('pe'), dep='dv', inc='dv')
            kb.step('tensor', lambda t, p=fin[qb % 2][fb:fb + dv, :], r_=fzb:
                    t.matmul(p, kb.bufs['ones_row'][:, 0:dv], r_, start=True, stop=True),
                    waits=kb.now('dv'), inc='pe')
            kb.step('vector', lambda v, o=finsb[fb:fb + dv, :], i=fin[qb % 2][fb:fb + dv, :]:
                    v.tensor_copy(o, i), waits=kb.now('pe'), dep='dv', inc='dv')
            out_cb(h, qb, av[0:dv, 0:512], finsb[fb:fb + dv, :], kb.now('dv'))
            av_free[(h, qb)] = kb.val('dv')
            qk_mark[h] = kb.val('pe')



L0A_LI = 0.8 - 0.6 * float(np.exp(-0.3))
L0A_SCL = float(1.0 / np.sqrt(32.0))


def emit_tm_store(kb, tag, src_fm, nfeat, dram_out, fence=True):
    ident = kb.bufs['ident_sb']
    tsb = kb.sbuf('tm_scr', [128, 512], BF16)
    ps1 = kb.bufs['ps1']
    if fence:
        kb.barrier('tensor', kb.now('dv', 'ac'))
    nk = nfeat // 128
    idx = 0
    for tt in range(T // 128):
        for k in range(nk):
            pslot = idx % 4
            ssem = 'st_tm%d' % pslot
            pview = ps1.ap().bitcast(BF16)[:, pslot * 1024:pslot * 1024 + 128]
            kb.step('tensor', lambda t, p=pview, i=src_fm(k)[:, tt * 128:(tt + 1) * 128]:
                    t.transpose(p, i, ident[:, :]),
                    waits=kb.now('dv') if idx >= 4 else [], inc='pe')
            pe_m = kb.val('pe')
            kb.step('vector', lambda v, o=tsb[:, pslot * 128:pslot * 128 + 128], i=pview:
                    v.tensor_copy(o, i), waits=[('pe', pe_m), (ssem, kb.val(ssem))],
                    dep='dv', inc='dv')
            dv_m = kb.val('dv')
            kb.step('sync', lambda e, o=dram_out[tt * 128:(tt + 1) * 128, k * 128:(k + 1) * 128],
                           i=tsb[:, pslot * 128:pslot * 128 + 128]:
                    e.dma_start(out=o, in_=i), waits=[('dv', dv_m)], inc=ssem, inc_n=16)
            idx += 1
    kb.step('gpsimd', lambda e: e.nop(),
            waits=[('st_tm%d' % i2, kb.val('st_tm%d' % i2)) for i2 in range(4)])


def emit_l0a_qkv(kb, x, xln, stats):
    nc = kb.nc
    qsh = kb.sbuf('a0_q', [64, 8 * T], BF16)
    kvf = kb.sbuf('a0_kv', [128, 8 * T], BF16)
    nrm = kb.sbuf('a0_nrm', [1, 64], F32)
    sq = stats[0:64, 2 * TH - T - 44:2 * TH - 44][0:64, 0:T]

    def ep_q(o, blk, ps_ap, waits):
        off, w = blk
        kb.step('vector', lambda v, o_=qsh[0:64, (2 * o) * T + off - HALO:(2 * o) * T + off - HALO + w],
                       i=ps_ap[0:64, :]:
                v.tensor_copy(o_, i), waits=waits, inc='dv')
        kb.step('vector', lambda v, o_=qsh[0:64, (2 * o + 1) * T + off - HALO:(2 * o + 1) * T + off - HALO + w],
                       i=ps_ap[64:128, :]:
                v.tensor_copy(o_, i), inc='dv')
    ep_q.engine = 'vector'
    emit_proj(kb, 'a0q', kb.dram['l0a_WqT'], DM, 512,
              lambda k, off, w: xln[:, k * TH + off: k * TH + off + w], ep_q)

    def ep_kv(o, blk, ps_ap, waits):
        off, w = blk
        kb.step('vector', lambda v, o_=kvf[:, o * T + off - HALO: o * T + off - HALO + w],
                       i=ps_ap:
                v.tensor_copy(o_, i), waits=waits, inc='dv')
    ep_kv.engine = 'vector'
    emit_proj(kb, 'a0kv', kb.dram['l0a_Wkv'], DM, 1024,
              lambda k, off, w: xln[:, k * TH + off: k * TH + off + w], ep_kv)
    kv_done = kb.now('dv')

    ps0 = kb.bufs['ps0']
    for sh in range(16):
        qv = qsh[(sh % 2) * 32:(sh % 2) * 32 + 32, (sh // 2) * T:(sh // 2) * T + T]
        kb.step('scalar', lambda s_, o=sq[0:32, :], i=qv:
                s_.activation(o, i, AF.Square), waits=kb.now('pe', 'dv'), inc='ac')
        for tb in range(4):
            kb.step('tensor', lambda t, p=ps0[0:1, tb * 512:(tb + 1) * 512],
                           l=kb.bufs['ones_col'][0:32, :], r_=sq[0:32, tb * 512:(tb + 1) * 512]:
                    t.matmul(p, l, r_, start=True, stop=True),
                    waits=kb.now('ac'), inc='pe' if tb == 3 else None)
        kb.step('vector', lambda v, o=nrm[0:1, sh:sh + 1], i=ps0[0:1, 0:T]:
                v.reduce_max(o, i, AX.X), waits=kb.now('pe'), dep='dv', inc='dv')
    for sh in range(16):
        kch, krow = sh // 4, (sh * 32) % 128
        base = 0 if krow < 64 else 64
        sel = krow - base
        kb.step('scalar', lambda s_, o=sq[0:64, :], i=kvf[base:base + 64, kch * T:(kch + 1) * T]:
                s_.activation(o, i, AF.Square), waits=kb.now('pe', 'dv'), inc='ac')
        for tb in range(4):
            kb.step('tensor', lambda t, p=ps0[0:1, tb * 512:(tb + 1) * 512],
                           l=kb.bufs['ones_col'][sel:sel + 32, :],
                           r_=sq[sel:sel + 32, tb * 512:(tb + 1) * 512]:
                    t.matmul(p, l, r_, start=True, stop=True),
                    waits=kb.now('ac'), inc='pe' if tb == 3 else None)
        kb.step('vector', lambda v, o=nrm[0:1, 16 + sh:17 + sh], i=ps0[0:1, 0:T]:
                v.reduce_max(o, i, AX.X), waits=kb.now('pe'), dep='dv', inc='dv')
    kb.step('scalar', lambda s_, o=nrm[0:1, 0:32], i=nrm[0:1, 0:32]:
            s_.activation(o, i, AF.Sqrt), waits=kb.now('dv'), inc='ac')
    kb.barrier('vector', kb.now('ac'))

    kb.step('sync', lambda e, o=kb.dram['l0a_Kfm_loc'].rearrange("(kc p) t -> p kc t", p=128),
                   i=kvf.ap().rearrange("p (kc t) -> p kc t", kc=8)[:, 0:4, :]:
            e.dma_start(out=o, in_=i), waits=kv_done, inc='st', inc_n=16)
    emit_tm_store(kb, 'vtm', lambda k: kvf[:, (4 + k) * T:(4 + k + 1) * T], 512,
                  kb.dram['l0a_Vtm_loc'])
    kb.step('sync', lambda e, o=kb.dram['l0a_nrm_loc'], i=nrm[0:1, 0:32]:
            e.dma_start(out=o, in_=i), waits=kb.now('ac', 'dv'), inc='st', inc_n=16)


def emit_l0a_attn(kb, x, xln, stats):
    nc = kb.nc
    qsh = kb.bufs['a0_q']
    osc = kb.sbuf('a0_o', [128, 4 * T], BF16)
    o0 = kb.sbuf('a0_o0', [64, T], F32)
    nrm = kb.sbuf('a0_n', [128, 64], F32)
    cst = kb.sbuf('a0_c', [1, 512], F32)
    lamb = kb.sbuf('a0_lamb', [128, 1], F32)
    gam = kb.sbuf('a0_gam', [64, 1], F32)
    ps0 = kb.bufs['ps0']

    kb.dma('sync', cst[0:1, 0:128], kb.dram['l0a_lam'], inc='ld2')
    kb.dma('sync', cst[0:1, 128:256], kb.dram['l0a_nrm_in'], inc='ld2')
    kb.dma('sync', gam[:, :], kb.dram['l0a_gammaT'], inc='ld2')
    ld2 = kb.val('ld2')
    kb.step('vector', lambda v, o=cst[0:1, 256:288], a=cst[0:1, 0:32], b=cst[0:1, 32:64]:
            v.tensor_tensor(o, a, b, ALU.mult), waits=[('ld2', ld2)], inc='dv')
    kb.step('vector', lambda v, o=cst[0:1, 288:320], a=cst[0:1, 64:96], b=cst[0:1, 96:128]:
            v.tensor_tensor(o, a, b, ALU.mult), dep='dv', inc='dv')
    kb.step('vector', lambda v, o=cst[0:1, 320:321], i=cst[0:1, 256:288]:
            v.reduce_sum(o, i, AX.X), dep='dv', inc='dv')
    kb.step('vector', lambda v, o=cst[0:1, 321:322], i=cst[0:1, 288:320]:
            v.reduce_sum(o, i, AX.X), dep='dv', inc='dv')
    kb.step('scalar', lambda s_, o=cst[0:1, 322:323], i=cst[0:1, 320:321]:
            s_.activation(o, i, AF.Exp), waits=kb.now('dv'), inc='ac')
    kb.step('scalar', lambda s_, o=cst[0:1, 323:324], i=cst[0:1, 321:322]:
            s_.activation(o, i, AF.Exp), dep='ac', inc='ac')
    kb.step('vector', lambda v, o=cst[0:1, 324:325], a=cst[0:1, 322:323], b=cst[0:1, 323:324]:
            v.tensor_tensor(o, a, b, ALU.subtract), waits=kb.now('ac'), inc='dv')
    kb.step('vector', lambda v, o=cst[0:1, 324:325], i=cst[0:1, 324:325]:
            v.tensor_scalar(o, i, L0A_LI, -1.0, ALU.add, ALU.mult), dep='dv', inc='dv')
    kb.step('tensor', lambda t, p=ps0[:, 0:1], r_=cst[0:1, 324:325]:
            t.matmul(p, kb.bufs['ones_row'][:, :], r_, start=True, stop=True),
            waits=kb.now('dv'), inc='pe')
    kb.step('vector', lambda v, o=lamb[:, :], i=ps0[:, 0:1]:
            v.tensor_copy(o, i), waits=kb.now('pe'), dep='dv', inc='dv')
    for step_i, (c0, c1) in enumerate([(0, 1), (2, 3)]):
        kb.step('vector', lambda v, o=cst[0:1, 326 + 16 * step_i:342 + 16 * step_i],
                       a=cst[0:1, 128 + 32 * c0 + 16:128 + 32 * c0 + 32],
                       b=cst[0:1, 128 + 32 * c1 + 16:128 + 32 * c1 + 32]:
                v.tensor_tensor(o, a, b, ALU.max), dep='dv', inc='dv')
    kb.step('vector', lambda v, o=cst[0:1, 326:342], a=cst[0:1, 326:342], b=cst[0:1, 342:358]:
            v.tensor_tensor(o, a, b, ALU.max), dep='dv', inc='dv')
    kb.dma('sync', cst[0:1, 358:374], kb.dram['l0a_qn'], inc='ld3')
    kb.step('vector', lambda v, o=cst[0:1, 374:390], a=cst[0:1, 358:374], b=cst[0:1, 326:342]:
            v.tensor_tensor(o, a, b, ALU.mult), waits=[('ld3', kb.val('ld3'))], dep='dv', inc='dv')
    kb.step('vector', lambda v, o=cst[0:1, 374:390], i=cst[0:1, 374:390]:
            v.tensor_scalar(o, i, -L0A_SCL, None, ALU.mult), dep='dv', inc='dv')
    for h in range(16):
        kb.step('tensor', lambda t, p=ps0[:, 2 + h:3 + h], r_=cst[0:1, 374 + h:375 + h]:
                t.matmul(p, kb.bufs['ones_row'][:, :], r_, start=True, stop=True),
                waits=kb.now('dv') if h == 0 else [], inc='pe' if h == 15 else None)
    kb.step('vector', lambda v, o=nrm[:, 0:16], i=ps0[:, 2:18]:
            v.tensor_copy(o, i), waits=kb.now('pe'), dep='dv', inc='dv')
    kb.step('vector', lambda v, o=nrm[0:64, 16:17], i=gam[:, :]:
            v.tensor_scalar(o, i, 1.0 - L0A_LI, None, ALU.mult), dep='dv', inc='dv')
    prep_done = kb.val('dv')

    def qap(h, qb):
        return qsh[(h % 2) * 32:(h % 2) * 32 + 32,
                   (h // 2) * T + qb * 512:(h // 2) * T + (qb + 1) * 512]

    def kld(h, buf, waits, sem):
        if getattr(kb, 'fused', False):
            for r in range(4):
                kb.step('sync', lambda e, o=buf[:, r * T:(r + 1) * T],
                               i=kb.dram['l0a_Kfm'][r * 512 + h * 32:r * 512 + (h + 1) * 32, :]:
                        e.dma_start(out=o, in_=i),
                        waits=(waits + [('dv', prep_done)]) if r == 0 else [],
                        inc=sem, inc_n=16)
        else:
            kb.step('sync', lambda e, o=buf, i=kb.dram['l0a_Kfm'][h * 32:(h + 1) * 32, :]:
                    e.dma_start(out=o, in_=i),
                    waits=waits + [('dv', prep_done)], inc=sem, inc_n=16)

    def vld(vh, buf, waits, sem):
        kb.step('gpsimd', lambda e,
                       o=buf.ap().rearrange("p (n d) -> p n d", d=65)[:, :, 0:64],
                       i=kb.dram['l0a_Vtm'][:, vh * 64:(vh + 1) * 64].rearrange(
                           "(n p) d -> p n d", p=128):
                e.dma_start(out=o, in_=i), waits=waits, inc=sem, inc_n=16)

    def out_cb(h, qb, num, finp, waits):
        vh, which = h // 2, h % 2
        orow = osc[(vh % 2) * 64:(vh % 2) * 64 + 64,
                   (vh // 2) * T + qb * 512:(vh // 2) * T + (qb + 1) * 512]
        if which == 0:
            kb.step('vector', lambda v, o=o0[:, qb * 512:(qb + 1) * 512], a=num, b=finp[0:64, :]:
                    v.tensor_tensor(o, a, b, ALU.mult), waits=waits, dep='dv', inc='dv')
        else:
            t1 = kb.bufs['scr_f32'][0:64, 512:1024]
            kb.step('vector', lambda v, o=t1, a=num, b=finp[0:64, :]:
                    v.tensor_tensor(o, a, b, ALU.mult), waits=waits, dep='dv', inc='dv')
            kb.step('vector', lambda v, o=t1, i=t1, m=lamb[0:64, 0:1],
                           p=o0[:, qb * 512:(qb + 1) * 512]:
                    v.scalar_tensor_tensor(o, i, m, p, ALU.mult, ALU.add), dep='dv', inc='dv')
            sq2 = kb.bufs['scr_f32'][0:64, 0:512]
            kb.step('scalar', lambda s_, o=sq2, i=t1:
                    s_.activation(o, i, AF.Square), waits=kb.now('dv'), inc='ac')
            kb.step('tensor', lambda t, p=kb.flash_fin[qb % 2][0:1, 0:512], r_=sq2:
                    t.matmul(p, kb.bufs['ones_col'][0:64, :], r_, start=True, stop=True),
                    waits=kb.now('ac'), inc='pe')
            ms = kb.bufs['scr_f32'][0:1, 0:512]
            kb.step('vector', lambda v, o=ms, i=kb.flash_fin[qb % 2][0:1, 0:512]:
                    v.tensor_scalar(o, i, 1.0 / 64, None, ALU.mult), waits=kb.now('pe'),
                    dep='dv', inc='dv')
            kb.step('scalar', lambda s_, o=ms, i=ms:
                    s_.activation(o, i, AF.Ln), waits=kb.now('dv'), inc='ac')
            kb.step('scalar', lambda s_, o=ms, i=ms:
                    s_.activation(o, i, AF.Exp, scale=0.5), dep='ac', inc='ac')
            kb.step('vector', lambda v, o=ms, i=ms:
                    v.tensor_scalar(o, i, 1e-8, None, ALU.add), waits=kb.now('ac'),
                    dep='dv', inc='dv')
            kb.step('vector', lambda v, o=ms, i=ms: v.reciprocal(o, i), dep='dv', inc='dv')
            kb.step('tensor', lambda t, p=kb.flash_fin[qb % 2][0:64, 0:512], r_=ms:
                    t.matmul(p, kb.bufs['ones_row'][:, 0:64], r_, start=True, stop=True),
                    waits=kb.now('dv'), inc='pe')
            kb.step('vector', lambda v, o=t1, a=t1, b=kb.flash_fin[qb % 2][0:64, 0:512]:
                    v.tensor_tensor(o, a, b, ALU.mult), waits=kb.now('pe'), dep='dv', inc='dv')
            kb.step('vector', lambda v, o=orow, i=t1, m=nrm[0:64, 16:17]:
                    v.tensor_scalar(o, i, m, None, ALU.mult), dep='dv', inc='dv')

    emit_flash_st(kb, 'a0', 16, 32, 64, qap, kld, vld,
                  lambda h: nrm[:, h:h + 1], lambda h: L0A_SCL, out_cb, vshare=2)

    def ep_o(o, blk, ps_ap, waits):
        off, w = blk
        kb.step('vector', lambda v, o_=x[:, o * TH + off: o * TH + off + w],
                       a=x[:, o * TH + off: o * TH + off + w], b=ps_ap:
                v.tensor_tensor(o_, a, b, ALU.add), waits=waits, dep='dv', inc='dv')
    ep_o.engine = 'vector'
    emit_proj(kb, 'a0o', kb.dram['l0a_WoutT'], 512, DM,
              lambda k, off, w: osc[:, k * T + off - HALO: k * T + off - HALO + w], ep_o)



LPAD = 4864
U0 = 3456
ND0 = 32

def l1b_union():
    out = []
    for qb in range(4):
        for kt in range(KTILES):
            sbs = [sb for sb in range(4) if ND0 <= kt - 4 * qb - sb <= 63]
            if sbs:
                out.append((qb, kt, sbs))
    return out


def emit_l1b_qkv(kb, x, xln, stats):
    nc = kb.nc
    qf = kb.sbuf('b1_q', [128, 8 * T], BF16)
    kf = kb.sbuf('b1_k', [128, 8 * T], BF16)
    nrm = kb.sbuf('b1_nrm', [1, 32], F32)
    sq = kb.sbuf('b1_sq', [64, T], F32)
    bia = kb.bufs['b1_bias']
    ps0 = kb.bufs['ps0']

    def mkep(dst, bofs, extra=()):
        def ep(o, blk, ps_ap, waits):
            off, w = blk
            kb.step('scalar', lambda s_, o_=dst[:, o * T + off - HALO: o * T + off - HALO + w],
                           i=ps_ap, b=bia[:, bofs + o:bofs + o + 1]:
                    s_.activation(o_, i, AF.Identity, bias=b), waits=waits + list(extra), inc='ac')
        ep.engine = 'scalar'
        return ep

    def norms(src, col0):
        for h in range(16):
            base = (h % 2) * 64
            kb.step('scalar', lambda s_, o=sq[0:64, :],
                           i=src[base:base + 64, (h // 2) * T:(h // 2 + 1) * T]:
                    s_.activation(o, i, AF.Square), waits=kb.now('pe', 'dv', 'ac'), inc='ac')
            for tb in range(4):
                kb.step('tensor', lambda t, p=ps0[0:1, tb * 512:(tb + 1) * 512],
                               l=kb.bufs['ones_col'][0:64, :], r_=sq[0:64, tb * 512:(tb + 1) * 512]:
                        t.matmul(p, l, r_, start=True, stop=True),
                        waits=kb.now('ac'), inc='pe' if tb == 3 else None)
            kb.step('vector', lambda v, o=nrm[0:1, col0 + h:col0 + h + 1], i=ps0[0:1, 0:T]:
                    v.reduce_max(o, i, AX.X), waits=kb.now('pe'), dep='dv', inc='dv')

    xv = lambda k, off, w: xln[:, k * TH + off: k * TH + off + w]
    emit_proj(kb, 'b1q', kb.dram['l1b_Wq'], DM, DM, xv, mkep(qf, 0))
    norms(qf, 0)
    emit_proj(kb, 'b1k', kb.dram['l1b_Wk'], DM, DM, xv, mkep(kf, 8))
    norms(kf, 16)
    kb.step('sync', lambda e, o=kb.dram['l1b_Kfm_loc'].rearrange("(kc p) t -> p kc t", p=128),
                   i=kf.ap().rearrange("p (kc t) -> p kc t", kc=8):
            e.dma_start(out=o, in_=i), waits=kb.now('dv', 'ac'), inc='stk', inc_n=16)
    stk = kb.val('stk')
    emit_proj(kb, 'b1v', kb.dram['l1b_Wv'], DM, DM, xv, mkep(kf, 16, extra=[('stk', stk)]))
    kb.step('scalar', lambda s_, o=nrm[0:1, 0:32], i=nrm[0:1, 0:32]:
            s_.activation(o, i, AF.Sqrt), dep='ac', inc='ac')
    kb.barrier('vector', kb.now('ac'))
    if 'l1b_qloc' in kb.dram:
        kb.step('sync', lambda e, o=kb.dram['l1b_qloc'], i=qf[:, :]:
                e.dma_start(out=o, in_=i), waits=kb.now('dv', 'ac'), inc='st', inc_n=16)
    emit_tm_store(kb, 'b1v', lambda k: kf[:, k * T:(k + 1) * T], 1024, kb.dram['l1b_Vtm_loc'])
    kb.step('sync', lambda e, o=kb.dram['l1b_nrm_loc'], i=nrm[0:1, :]:
            e.dma_start(out=o, in_=i), inc='st', inc_n=16)


def emit_l1b_attn(kb, x, xln, stats):
    nc = kb.nc
    RXC = (16 * LPAD) // 128
    qf = kb.bufs['b1_qv']
    rx = kb.sbuf('b1_rx', [128, RXC], F32)
    rcm = kb.sbuf('b1_rcm', [128, RXC], F32)
    rmk = kb.sbuf('b1_rm', [128, RXC], F32)
    he = [kb.sbuf('b1_he%d' % i, [128, (64 - ND0) * 128], BF16) for i in range(2)]
    j128 = kb.bufs['j128_sb']
    cst = kb.sbuf('b1_c', [1, 512], F32)
    nrm = kb.sbuf('b1_n', [128, 16], F32)
    dmx = kb.sbuf('b1_dm', [128, 1], F32)
    ps0 = kb.bufs['ps0']
    e8 = kb.dram_scratch('l1b_e8', [16, LPAD], BF16)
    dbounce = kb.dram_scratch('l1b_db', [1, 128], F32)

    kb.dma('sync', rx[:, :], kb.dram['l1b_relx'], inc='ld2')
    kb.dma('sync', rcm[:, :], kb.dram['l1b_relcm'], inc='ld2')
    kb.dma('sync', rmk[:, :], kb.dram['l1b_rmask'], inc='ld2')
    kb.dma('sync', cst[0:1, 0:128], kb.dram['l1b_nrm_in'], inc='ld2')
    kb.dma('sync', cst[0:1, 128:144], kb.dram['l1b_qn'], inc='ld2')
    ld2 = kb.val('ld2')
    kb.step('vector', lambda v, o=rmk[:, :], a=rcm[:, :], b=rmk[:, :]:
            v.tensor_tensor(o, a, b, ALU.mult), waits=[('ld2', ld2)], inc='dv')
    kb.step('vector', lambda v, o=rx[:, :], i=rx[:, :]:
            v.tensor_scalar(o, i, 8.0, None, ALU.mult), inc='dv')
    kb.step('vector', lambda v, o=rx[:, :], i=rmk[:, :], m=-8.0, p=rx[:, :]:
            v.scalar_tensor_tensor(o, i, m, p, ALU.mult, ALU.add), dep='dv', inc='dv')
    kb.step('vector', lambda v, o=dmx[:, 0:1], i=rx[:, :]:
            v.reduce_max(o, i, AX.X, apply_absolute_value=True), dep='dv', inc='dv')
    kb.step('gpsimd', lambda e, o=e8.rearrange("h u -> (h u)").rearrange("(p c) -> p c", p=128),
                   i=rx[:, :]:
            e.dma_start(out=o, in_=i), waits=kb.now('dv'), inc='ldg2', inc_n=16)
    kb.step('sync', lambda e, o=dbounce.rearrange("one p -> p one"), i=dmx[:, 0:1]:
            e.dma_start(out=o, in_=i), waits=kb.now('dv'), inc='ld3', inc_n=16)
    kb.step('sync', lambda e, o=cst[0:1, 144:272], i=dbounce:
            e.dma_start(out=o, in_=i), dep='ld3', inc='ld3', inc_n=16)
    kb.step('vector', lambda v, o=cst[0:1, 272:273], i=cst[0:1, 144:272]:
            v.reduce_max(o, i, AX.X), waits=[('ld3', kb.val('ld3'))], dep='dv', inc='dv')
    for si, (c0, c1) in enumerate([(0, 1), (2, 3)]):
        kb.step('vector', lambda v, o=cst[0:1, 280 + 16 * si:296 + 16 * si],
                       a=cst[0:1, 32 * c0 + 16:32 * c0 + 32],
                       b=cst[0:1, 32 * c1 + 16:32 * c1 + 32]:
                v.tensor_tensor(o, a, b, ALU.max), dep='dv', inc='dv')
    kb.step('vector', lambda v, o=cst[0:1, 280:296], a=cst[0:1, 280:296], b=cst[0:1, 296:312]:
            v.tensor_tensor(o, a, b, ALU.max), dep='dv', inc='dv')
    kb.step('vector', lambda v, o=cst[0:1, 312:328], a=cst[0:1, 128:144], b=cst[0:1, 280:296]:
            v.tensor_tensor(o, a, b, ALU.mult), dep='dv', inc='dv')
    kb.step('vector', lambda v, o=cst[0:1, 312:328], i=cst[0:1, 312:328],
                   m=cst[0:1, 272:273]:
            v.tensor_scalar(o, i, m, -0.125, ALU.add, ALU.mult), dep='dv', inc='dv')
    for h in range(16):
        kb.step('tensor', lambda t, p=ps0[:, 2 + h:3 + h], r_=cst[0:1, 312 + h:313 + h]:
                t.matmul(p, kb.bufs['ones_row'][:, :], r_, start=True, stop=True),
                waits=kb.now('dv') if h == 0 else [], inc='pe' if h == 15 else None)
    kb.step('vector', lambda v, o=nrm[:, 0:16], i=ps0[:, 2:18]:
            v.tensor_copy(o, i), waits=kb.now('pe'), dep='dv', inc='dv')
    prep_done = kb.val('dv')

    union = l1b_union()

    def qap(h, qb):
        return qf[(h % 2) * 64:(h % 2) * 64 + 64,
                  (h // 2) * T + qb * 512:(h // 2) * T + (qb + 1) * 512]

    from concourse.ap import AP as _AP
    e8h = e8.tensor if hasattr(e8, 'tensor') else None

    def kld_full(h, buf, waits, sem):
        if getattr(kb, 'fused', False):
            for r in range(4):
                kb.step('sync', lambda e, o=buf[:, r * T:(r + 1) * T],
                               i=kb.dram['l1b_Kfm'][r * 1024 + h * 64:r * 1024 + (h + 1) * 64, :]:
                        e.dma_start(out=o, in_=i),
                        waits=(waits + [('dv', prep_done), ('ldg2', kb.val('ldg2'))])
                              if r == 0 else [],
                        inc=sem, inc_n=16)
        else:
            kb.step('sync', lambda e, o=buf, i=kb.dram['l1b_Kfm'][h * 64:(h + 1) * 64, :]:
                    e.dma_start(out=o, in_=i),
                    waits=waits + [('dv', prep_done), ('ldg2', kb.val('ldg2'))],
                    inc=sem, inc_n=16)
        hb = he[h % 2]
        for ni, n in enumerate(range(ND0, 64)):
            u0 = 128 * n - 127 - U0
            src = _AP(tensor=e8.tensor, offset=h * LPAD + u0, ap=[[1, 128], [1, 128]])
            kb.step('sync', lambda e, o=hb[:, ni * 128:(ni + 1) * 128], i=src:
                    e.dma_start(out=o, in_=i), inc=sem, inc_n=16)

    def vld(vh, buf, waits, sem):
        kb.step('gpsimd', lambda e,
                       o=buf.ap().rearrange("p (n d) -> p n d", d=65)[:, :, 0:64],
                       i=kb.dram['l1b_Vtm'][:, vh * 64:(vh + 1) * 64].rearrange(
                           "(n p) d -> p n d", p=128):
                e.dma_start(out=o, in_=i), waits=waits, inc=sem, inc_n=16)

    def bias_mms(h, qb, kt, ps_ap):
        fns = []
        for sb in range(4):
            n = kt - 4 * qb - sb
            if ND0 <= n <= 63:
                ni = n - ND0
                fns.append(lambda t, l=he[h % 2][:, ni * 128:(ni + 1) * 128],
                                  p=ps_ap[:, sb * 128:(sb + 1) * 128]:
                           t.matmul(p, l, j128[:, :], start=False, stop=True,
                                    skip_group_check=True))
        return fns

    def out_cb(h, qb, num, finsb, waits):
        base = (h % 2) * 64
        t1 = kb.bufs['scr_f32'][base:base + 64, 512:1024]
        xsl = x[base:base + 64, (h // 2) * TH + HALO + qb * 512:(h // 2) * TH + HALO + (qb + 1) * 512]
        kb.step('vector', lambda v, o=t1, a=num, b=finsb:
                v.tensor_tensor(o, a, b, ALU.mult), waits=waits, dep='dv', inc='dv')
        kb.step('vector', lambda v, o=xsl, a=xsl, b=t1:
                v.tensor_tensor(o, a, b, ALU.add), dep='dv', inc='dv')

    emit_flash_st(kb, 'b1', 16, 64, 64, qap, kld_full, vld,
                  lambda h: nrm[:, h:h + 1], lambda h: 0.125, out_cb,
                  bias_mms=bias_mms, vshare=1, fin_base=lambda h: (h % 2) * 64)



YAT_EPS = 1e-5
YAT_LS0 = float(np.log(np.sqrt(np.float32(1024.0)) / np.log(np.float32(1025.0))))


def emit_l1a_qkv(kb, x, xln, stats):
    nc = kb.nc
    qf = kb.sbuf('y1_q', [128, 8 * T], BF16)
    kf = kb.sbuf('y1_k', [128, 8 * T], BF16)
    nrm = kb.sbuf('y1_nrm', [128, 256], F32)
    wn2 = kb.sbuf('y1_wn', [128, 8], F32)
    sal = kb.sbuf('y1_sal', [1, 8], F32)
    salb = kb.sbuf('y1_salb', [128, 4], F32)
    bsal = kb.sbuf('y1_bs', [128, 8], F32)
    xn2b = kb.sbuf('y1_xb', [128, T], F32)
    wtmp = kb.sbuf('y1_wt', [128, 1024], BF16)
    sqw = kb.sbuf('y1_sqw', [128, 1024], BF16)
    den = kb.sbuf('y1_den', [128, 512], F32)
    sqd = kb.sbuf('y1_sqd', [128, 512], F32)
    qnh = stats[0:1, 0:2 * T]
    sq = None
    ps0, ps1 = kb.bufs['ps0'], kb.bufs['ps1']
    nbounce = kb.dram_scratch('l1a_nb', [T], F32)

    kb.step('vector', lambda v, o=qnh[0:1, T:2 * T], i=stats[32:33, HALO:HALO + T]:
            v.tensor_copy(o, i), waits=kb.now('dv', 'ac'), inc='dv')
    kb.step('vector', lambda v, o=qnh[0:1, 0:T], a=qnh[0:1, T:2 * T], b=qnh[0:1, T:2 * T]:
            v.tensor_tensor(o, a, b, ALU.mult), dep='dv', inc='dv')
    kb.step('vector', lambda v, o=qnh[0:1, T:2 * T], i=stats[64:65, HALO:HALO + T]:
            v.tensor_copy(o, i), dep='dv', inc='dv')
    kb.step('vector', lambda v, o=qnh[0:1, 0:T], a=qnh[0:1, 0:T],
                   b=qnh[0:1, T:2 * T]:
            v.tensor_tensor(o, a, b, ALU.mult), dep='dv', inc='dv')
    kb.step('vector', lambda v, o=qnh[0:1, 0:T], i=qnh[0:1, 0:T]:
            v.tensor_scalar(o, i, float(DM), None, ALU.mult), dep='dv', inc='dv')
    for tb in range(4):
        kb.step('tensor', lambda t, p=ps0[:, tb * 512:(tb + 1) * 512],
                       r_=qnh[0:1, tb * 512:(tb + 1) * 512]:
                t.matmul(p, kb.bufs['ones_row'][:, :], r_, start=True, stop=True),
                waits=kb.now('dv') if tb == 0 else [], inc='pe' if tb == 3 else None)
    kb.step('vector', lambda v, o=xn2b[:, :], i=ps0[:, 0:T]:
            v.tensor_copy(o, i), waits=kb.now('pe'), dep='dv', inc='dv')
    kb.step('scalar', lambda s_, o=sal[0:1, 0:3], i=kb.bufs['y1_alpha'][0:1, 0:3]:
            s_.activation(o, i, AF.Exp, scale=YAT_LS0), inc='ac')
    kb.step('tensor', lambda t, p=ps1[:, 8:11], r_=sal[0:1, 0:3]:
            t.matmul(p, kb.bufs['ones_row'][:, :], r_, start=True, stop=True),
            waits=kb.now('ac'), inc='pe')
    kb.step('vector', lambda v, o=salb[:, 0:3], i=ps1[:, 8:11]:
            v.tensor_copy(o, i), waits=kb.now('pe'), dep='dv', inc='dv')
    kb.barrier('tensor', kb.now('dv'))

    def wn2pass(wkey, ai, bofs):
        for o in range(8):
            kb.dma('gpsimd', wtmp.ap().rearrange("p (kc o) -> p kc o", kc=8),
                   kb.dram[wkey][:, o * 128:(o + 1) * 128].rearrange(
                       "(kc p) o -> p kc o", p=128),
                   waits=kb.now('pe') if o else kb.now('pe', 'ac'), inc='ldw')
            ldw = kb.val('ldw')
            kb.step('scalar', lambda s_, o_=sqw[:, :], i=wtmp[:, :]:
                    s_.activation(o_, i, AF.Square), waits=[('ldw', ldw)] + kb.now('pe'),
                    inc='ac')
            for k2 in range(8):
                kb.step('tensor', lambda t, p=ps1[0:128, o:o + 1],
                               l=sqw[:, k2 * 128:(k2 + 1) * 128],
                               r_=kb.bufs['ones_col_bf'][:, 0:1], f=(k2 == 0), lst=(k2 == 7):
                        t.matmul(p, l, r_, start=f, stop=lst, skip_group_check=True),
                        waits=kb.now('ac'), inc='pe' if k2 == 7 else None)
        kb.step('vector', lambda v, o=wn2[:, 0:8], i=ps1[0:128, 0:8]:
                v.tensor_copy(o, i), waits=kb.now('pe'), dep='dv', inc='dv')
        kb.step('vector', lambda v, o=bsal[:, 0:8], i=kb.bufs['y1_bias'][:, bofs:bofs + 8],
                       m=salb[:, ai:ai + 1]:
                v.tensor_scalar(o, i, m, None, ALU.mult), waits=kb.now('ac'), dep='dv',
                inc='dv')
        kb.barrier('tensor', kb.now('dv'))

    def yat_ep(dst, ai):
        def ep(o, blk, ps_ap, waits):
            off, w = blk
            t0 = off - HALO
            kb.step('vector', lambda v, o_=den[:, 0:w], i=ps_ap, m=-2.0,
                           p=xn2b[:, t0:t0 + w]:
                    v.scalar_tensor_tensor(o_, i, m, p, ALU.mult, ALU.add),
                    waits=waits, dep='dv', inc='dv')
            kb.step('vector', lambda v, o_=den[:, 0:w], i=den[:, 0:w], m=wn2[:, o:o + 1]:
                    v.tensor_scalar(o_, i, m, YAT_EPS, ALU.add, ALU.add), dep='dv', inc='dv')
            kb.step('scalar', lambda s_, o_=sqd[:, 0:w], i=ps_ap:
                    s_.activation(o_, i, AF.Square), waits=kb.now('dv'), inc='ac')
            kb.step('vector', lambda v, o_=den[:, 0:w], i=den[:, 0:w]:
                    v.reciprocal(o_, i), dep='dv', inc='dv')
            kb.step('vector', lambda v, o_=den[:, 0:w], a=sqd[:, 0:w], b=den[:, 0:w]:
                    v.tensor_tensor(o_, a, b, ALU.mult), waits=kb.now('ac'), dep='dv',
                    inc='dv')
            kb.step('vector', lambda v, o_=dst[:, o * T + t0:o * T + t0 + w], i=den[:, 0:w],
                           m=salb[:, ai:ai + 1], m2=bsal[:, o:o + 1]:
                    v.tensor_scalar(o_, i, m, m2, ALU.mult, ALU.add), dep='dv', inc='dv')
        ep.engine = 'vector'
        return ep

    xv = lambda k, off, w: xln[:, k * TH + off: k * TH + off + w]
    wn2pass('l1a_wq', 0, 0)
    emit_proj(kb, 'y1q', kb.dram['l1a_wq'], DM, DM, xv, yat_ep(qf, 0))
    wn2pass('l1a_wk', 1, 8)
    emit_proj(kb, 'y1k', kb.dram['l1a_wk'], DM, DM, xv, yat_ep(kf, 1))

    sq = stats[64:128, 2 * TH - T - 44:2 * TH - 44][:, 0:T]
    for h in range(16):
        base = (h % 2) * 64
        for which, src in ((0, qf), (1, kf)):
            kb.step('scalar', lambda s_, o=sq[0:64, :],
                           i=src[base:base + 64, (h // 2) * T:(h // 2 + 1) * T]:
                    s_.activation(o, i, AF.Square), waits=kb.now('pe', 'dv', 'ac'), inc='ac')
            for tb in range(4):
                kb.step('tensor', lambda t, p=ps0[0:1, tb * 512:(tb + 1) * 512],
                               l=kb.bufs['ones_col'][64:128, :],
                               r_=sq[0:64, tb * 512:(tb + 1) * 512],
                               f=(which == 0), lst=(which == 1):
                        t.matmul(p, l, r_, start=f, stop=lst, skip_group_check=True),
                        waits=kb.now('ac'), inc='pe' if tb == 3 else None)
        kb.step('vector', lambda v, o=qnh[0:1, T:2 * T], i=ps0[0:1, 0:T]:
                v.tensor_copy(o, i), waits=kb.now('pe'), dep='dv', inc='dv')
        kb.step('sync', lambda e, o=nbounce, i=qnh[0:1, T:2 * T]:
                e.dma_start(out=o, in_=i), waits=kb.now('dv'), inc='ld4', inc_n=16)
        for qt in range(16):
            kb.step('sync', lambda e, o=nrm[:, h * 16 + qt:h * 16 + qt + 1],
                           i=nbounce[qt * 128:(qt + 1) * 128].rearrange("(p one) -> p one", one=1):
                    e.dma_start(out=o, in_=i), dep='ld4' if qt == 0 else None,
                    inc='ld4', inc_n=16)
        kb.barrier('tensor', [('ld4', kb.val('ld4'))])
    kb.step('sync', lambda e, o=kb.dram['l1a_qloc'], i=qf[:, :]:
            e.dma_start(out=o, in_=i), waits=kb.now('dv', 'ac') + [('ld4', kb.val('ld4'))],
            inc='st', inc_n=16)
    kb.step('sync', lambda e, o=kb.dram['l1a_Kfm_loc'].rearrange("(kc p) t -> p kc t", p=128),
                   i=kf.ap().rearrange("p (kc t) -> p kc t", kc=8):
            e.dma_start(out=o, in_=i), inc='stk', inc_n=16)
    kb.step('sync', lambda e, o=kb.dram['l1a_cn'], i=nrm[:, 0:256]:
            e.dma_start(out=o, in_=i), inc='st', inc_n=16)
    stk = kb.val('stk')
    kb.barrier('vector', [('stk', stk)])
    wn2pass('l1a_wv', 2, 16)

    emit_proj(kb, 'y1v', kb.dram['l1a_wv'], DM, DM, xv, yat_ep(kf, 2))
    emit_tm_store(kb, 'y1v', lambda k: kf[:, k * T:(k + 1) * T], 1024, kb.dram['l1a_Vtm_loc'])


def emit_l1a_attn(kb, x, xln, stats):
    nc = kb.nc
    qbuf = kb.sbuf('y1_qb', [128, T], BF16)
    osc = kb.sbuf('y1_osc', [128, 8 * T], BF16)
    cn = kb.sbuf('y1_cn', [128, 256], F32)
    cne = kb.sbuf('y1_cne', [128, 256], F32)
    alf = kb.sbuf('y1_alf', [128, 4], F32)
    rsb = kb.sbuf('y1_r', [128, SEQ], F32)
    usb = kb.sbuf('y1_u', [128, 2048], F32)
    psb = kb.sbuf('y1_p', [128, SEQ], BF16)
    ptb = kb.sbuf('y1_pt', [128, 512], BF16)
    fzb = kb.sbuf('y1_fz', [1, 512], F32)
    fnb = kb.sbuf('y1_fn', [128, 512], F32)
    ksb = kb.sbuf('y1_kf', [128, SEQ], BF16)
    vsb = kb.sbuf('y1_vt', [128, KTILES * 65], BF16)
    ident = kb.bufs['ident_sb']
    ps0, ps1 = kb.bufs['ps0'], kb.bufs['ps1']
    sps = [ps0[:, 0:1024], ps0[:, 1024:2048]]
    avp = ps1[:, 0:512]
    fnp = ps1[:, 512:1024]
    trp = [ps1.ap().bitcast(BF16)[:, 2048 + i * 128:2048 + (i + 1) * 128] for i in range(4)]

    kb.dma('sync', cn[:, :], kb.dram['l1a_cn'], inc='ld2')
    ld2 = kb.val('ld2')
    kb.step('vector', lambda v, o=cne[:, :], i=cn[:, :]:
            v.tensor_scalar(o, i, YAT_EPS, None, ALU.add), waits=[('ld2', ld2)], inc='dv')
    kb.step('vector', lambda v, o=vsb.ap().rearrange("p (n d) -> p n d", d=65)[:, :, 64:65]:
            v.memset(o, 1.0), inc='dv')
    prep = kb.val('dv')

    h_mark = {}
    for h in range(16):
        base = (h % 2) * 64
        if getattr(kb, 'fused', False):
            for r in range(4):
                kb.step('sync', lambda e, o=ksb[base:base + 64, r * T:(r + 1) * T],
                               i=kb.dram['l1a_Kfm'][r * 1024 + h * 64:r * 1024 + (h + 1) * 64, :]:
                        e.dma_start(out=o, in_=i),
                        waits=(([('pe', h_mark[h - 1])] if h >= 1 else [('dv', prep)])
                               if r == 0 else []),
                        inc='ldk0', inc_n=16)
        else:
            kb.step('sync', lambda e, o=ksb[base:base + 64, :],
                           i=kb.dram['l1a_Kfm'][h * 64:(h + 1) * 64, :]:
                    e.dma_start(out=o, in_=i),
                    waits=([('pe', h_mark[h - 1])] if h >= 1 else [('dv', prep)]),
                    inc='ldk0', inc_n=16)
        kb.step('sync', lambda e, o=qbuf[base:base + 64, :],
                       i=kb.dram['l1a_qloc'][(h % 2) * 64:(h % 2) * 64 + 64,
                                             (h // 2) * T:(h // 2 + 1) * T]:
                e.dma_start(out=o, in_=i), inc='ldk0', inc_n=16)
        ldk = kb.val('ldk0')
        kb.step('gpsimd', lambda e,
                       o=vsb.ap().rearrange("p (n d) -> p n d", d=65)[:, :, 0:64],
                       i=kb.dram['l1a_Vtm'][:, h * 64:(h + 1) * 64].rearrange(
                           "(n p) d -> p n d", p=128):
                e.dma_start(out=o, in_=i),
                waits=([('pe', h_mark[h - 1])] if h >= 1 else [('dv', prep)]),
                inc='ldv0', inc_n=16)
        ldv = kb.val('ldv0')
        for qtg in range(4):
            exp_marks = {}
            for qi in range(4):
                qt = qtg * 4 + qi
                q_ap = qbuf[base:base + 64, qt * 128:(qt + 1) * 128]
                ccol = cne[:, h * 16 + qt:h * 16 + qt + 1]
                sq_marks = {}
                for rr in range(8):
                    sp = sps[rr % 2]
                    for i2 in range(2):
                        k0 = rr * 1024 + i2 * 512
                        w0 = []
                        if i2 == 0:
                            w0 = [('ac', sq_marks[rr - 2])] if rr >= 2 else \
                                 (kb.now('ac') if rr == 0 else [])
                            if rr == 0 and qt == 0 and qtg == 0:
                                w0 += [('ldk0', ldk), ('ldv0', ldv)]
                        kb.step('tensor', lambda t, p=sp[:, i2 * 512:(i2 + 1) * 512],
                                       l=q_ap, r_=ksb[base:base + 64, k0:k0 + 512]:
                                t.matmul(p, l, r_, start=True, stop=True),
                                waits=w0, inc='pe' if i2 == 1 else None)
                    pe_s = kb.val('pe')
                    kb.step('scalar', lambda s_, o=usb[:, 0:1024], i=sp:
                            s_.activation(o, i, AF.Square),
                            waits=[('pe', pe_s)] + kb.now('dv'), inc='ac')
                    sq_marks[rr] = kb.val('ac')
                    kb.step('vector', lambda v, o=rsb[:, rr * 1024:(rr + 1) * 1024],
                                   i=usb[:, 0:1024], m=-2.0, m2=ccol:
                            v.tensor_scalar(o, i, m, m2, ALU.mult, ALU.add),
                            waits=[('ac', sq_marks[rr])], dep='dv', inc='dv')
                    kb.step('vector', lambda v, o=rsb[:, rr * 1024:(rr + 1) * 1024],
                                   i=rsb[:, rr * 1024:(rr + 1) * 1024]:
                            v.reciprocal(o, i), dep='dv', inc='dv')
                kb.step('vector', lambda v, o=alf[:, 1:2], i=rsb[:, :]:
                        v.reduce_max(o, i, AX.X), dep='dv', inc='dv')
                kb.step('vector', lambda v, o=alf[:, 0:1], i=ccol:
                        v.tensor_scalar(o, i, 0.5, None, ALU.mult), dep='dv', inc='dv')
                kb.step('vector', lambda v, o=alf[:, 2:3], a=alf[:, 0:1], b=alf[:, 1:2]:
                        v.tensor_tensor(o, a, b, ALU.mult), dep='dv', inc='dv')
                kb.step('vector', lambda v, o=alf[:, 2:3], i=alf[:, 2:3]:
                        v.tensor_scalar(o, i, -1.0, None, ALU.mult), dep='dv', inc='dv')
                dv_m = kb.val('dv')
                kb.step('scalar', lambda s_, o=psb[:, :], i=rsb[:, :],
                               b=alf[:, 2:3], m=alf[:, 0:1]:
                        s_.activation(o, i, AF.Exp, bias=b, scale=m),
                        waits=[('dv', dv_m)] + kb.now('pe'), inc='ac')
                exp_marks[qi] = kb.val('ac')
                for kt in range(KTILES):
                    slot = kt % 4
                    kb.step('tensor', lambda t, p=trp[slot], i=psb[:, kt * 128:(kt + 1) * 128]:
                            t.transpose(p, i, ident[:, :]),
                            waits=([('ac', exp_marks[qi])] if kt == 0 else []) +
                                  (kb.now('dv') if kt >= 4 and slot == 0 else []) +
                                  (kb.now('dv') if kt == 0 and qi == 0 and (h, qtg) != (0, 0) else []),
                            inc='pe')
                    pe_t = kb.val('pe')
                    kb.step('vector', lambda v, o=ptb[:, slot * 128:(slot + 1) * 128], i=trp[slot]:
                            v.tensor_copy(o, i), waits=[('pe', pe_t)], dep='dv', inc='dv')
                    dv_t = kb.val('dv')
                    kb.step('tensor', lambda t, p=avp[0:65, qi * 128:(qi + 1) * 128],
                                   l=vsb[:, kt * 65:(kt + 1) * 65],
                                   r_=ptb[:, slot * 128:(slot + 1) * 128], f=(kt == 0):
                            t.matmul(p, l, r_, start=f, stop=(kt == KTILES - 1),
                                     skip_group_check=True),
                            waits=[('dv', dv_t)], inc='pe' if kt == KTILES - 1 else None)
            kb.step('vector', lambda v, o=fzb[0:1, :], i=avp[64:65, 0:512]:
                    v.reciprocal(o, i), waits=kb.now('pe'), dep='dv', inc='dv')
            kb.step('tensor', lambda t, p=fnp[base:base + 64, :], r_=fzb[0:1, :]:
                    t.matmul(p, kb.bufs['ones_row'][:, 0:64], r_, start=True, stop=True),
                    waits=kb.now('dv'), inc='pe')
            kb.step('vector', lambda v, o=fnb[base:base + 64, :],
                           i=fnp[base:base + 64, :]:
                    v.tensor_copy(o, i), waits=kb.now('pe'), dep='dv', inc='dv')
            kb.step('vector', lambda v,
                           o=osc[base:base + 64, (h // 2) * T + qtg * 512:(h // 2) * T + (qtg + 1) * 512],
                           a=avp[0:64, :], b=fnb[base:base + 64, :]:
                    v.tensor_tensor(o, a, b, ALU.mult), dep='dv', inc='dv')
            h_mark[h] = kb.val('pe')

    def ep_o(o, blk, ps_ap, waits):
        off, w = blk
        kb.step('vector', lambda v, o_=x[:, o * TH + off: o * TH + off + w], i=ps_ap,
                       m=kb.bufs['y1_bo'][:, o:o + 1], p=x[:, o * TH + off: o * TH + off + w]:
                v.scalar_tensor_tensor(o_, i, m, p, ALU.add, ALU.add),
                waits=waits, dep='dv', inc='dv')
    ep_o.engine = 'vector'
    emit_proj(kb, 'y1o', kb.dram['l1a_wo'], DM, DM,
              lambda k, off, w: osc[:, k * T + off - HALO: k * T + off - HALO + w], ep_o)



SSD_H = 8
SSD_P = 256
SSD_N = 16
SSD_EPS = 1e-5


def emit_ssd_consts(kb, scr=None):
    nc = kb.nc
    la = kb.sbuf('sd_la', [1, 8], F32)
    abc = kb.sbuf('sd_abc', [128, 8], F32)
    dec = kb.sbuf('sd_dec', [128, 8], F32)
    rbc = kb.sbuf('sd_rbc', [16, 2048], F32)
    lmt = kb.sbuf('sd_lmt', [128, 8 * 64], F32)
    eab = kb.sbuf('sd_eab', [16, 8 * 64], F32)
    t8 = kb.sbuf('sd_t8', [1, 8], F32)
    ps0 = kb.bufs['ps0']
    kb.barrier('tensor', kb.now('dv', 'ac'))
    kb.dma('sync', la[0:1, :], kb.dram['ssd_logA'], waits=kb.now('dv', 'ac'), inc='ld5')
    kb.step('scalar', lambda s_, o=la[0:1, :], i=la[0:1, :]:
            s_.activation(o, i, AF.Exp), waits=[('ld5', kb.val('ld5'))], inc='ac')
    kb.step('vector', lambda v, o=la[0:1, :], i=la[0:1, :]:
            v.tensor_scalar(o, i, -1.0, None, ALU.mult), waits=kb.now('ac'), inc='dv')
    kb.step('tensor', lambda t, p=ps0[:, 0:8], r_=la[0:1, :]:
            t.matmul(p, kb.bufs['ones_row'][:, :], r_, start=True, stop=True),
            waits=kb.now('dv'), inc='pe')
    kb.step('vector', lambda v, o=abc[:, :], i=ps0[:, 0:8]:
            v.tensor_copy(o, i), waits=kb.now('pe'), dep='dv', inc='dv')
    kb.step('vector', lambda v, o=dec[:, :], i=abc[:, :], m=kb.bufs['sd_pos'][:, 0:1]:
            v.tensor_scalar(o, i, m, None, ALU.mult), dep='dv', inc='dv')
    kb.step('scalar', lambda s_, o=dec[:, :], i=dec[:, :]:
            s_.activation(o, i, AF.Exp), waits=kb.now('dv'), inc='ac')
    kb.step('vector', lambda v, o=t8[0:1, :], i=la[0:1, :]:
            v.tensor_scalar(o, i, 64.0, None, ALU.mult), dep='dv', inc='dv')
    kb.step('scalar', lambda s_, o=t8[0:1, :], i=t8[0:1, :]:
            s_.activation(o, i, AF.Exp), waits=kb.now('dv'), inc='ac')
    rb = kb.dram_scratch('ssd_rb', [8], F32)
    kb.step('sync', lambda e, o=rb, i=t8[0:1, :]: e.dma_start(out=o, in_=i),
            waits=kb.now('ac'), inc='ld5', inc_n=16)
    laT = kb.sbuf('sd_laT', [40, 1], F32)
    kb.step('sync', lambda e, o=laT[32:40, 0:1], i=rb.rearrange("(h one) -> h one", one=1):
            e.dma_start(out=o, in_=i), dep='ld5', inc='ld5', inc_n=16)
    if scr is None:
        scr = kb.bufs['stats_sb']
    sel = scr[32:40, 0:2048]
    r1 = scr[0:1, 0:2048]
    kb.dma('sync', sel, kb.dram['sd_sel8_d'], waits=kb.now('dv', 'ac'), inc='ld5')
    for qq in range(4):
        kb.step('tensor', lambda t, p=kb.bufs['ps1'][0:1, qq * 512:(qq + 1) * 512],
                       l=laT[32:40, 0:1], r_=sel[:, qq * 512:(qq + 1) * 512]:
                t.matmul(p, l, r_, start=True, stop=True),
                waits=[('ld5', kb.val('ld5'))] if qq == 0 else [],
                inc='pe' if qq == 3 else None)
    kb.step('vector', lambda v, o=r1, i=kb.bufs['ps1'][0:1, 0:2048]:
            v.tensor_copy(o, i), waits=kb.now('pe'), dep='dv', inc='dv')
    for qq in range(4):
        kb.step('tensor', lambda t, p=ps0[0:16, qq * 512:(qq + 1) * 512],
                       r_=r1[0:1, qq * 512:(qq + 1) * 512]:
                t.matmul(p, kb.bufs['ones_row'][:, 0:16], r_, start=True, stop=True),
                waits=kb.now('dv') if qq == 0 else [],
                inc='pe' if qq == 3 else None)
    kb.step('vector', lambda v, o=rbc[:, :], i=ps0[0:16, 0:2048]:
            v.tensor_copy(o, i), waits=kb.now('pe'), dep='dv', inc='dv')
    for h in range(8):
        kb.step('scalar', lambda s_, o=lmt[:, h * 64:(h + 1) * 64], i=kb.bufs['sd_ldif'][:, 0:64],
                       m=abc[:, h:h + 1]:
                s_.activation(o, i, AF.Exp, scale=m), waits=kb.now('dv'), dep='ac', inc='ac')
        kb.step('vector', lambda v, o=lmt[:, h * 64:(h + 1) * 64], a=lmt[:, h * 64:(h + 1) * 64],
                       b=kb.bufs['sd_mask'][:, 0:64]:
                v.tensor_tensor(o, a, b, ALU.mult), waits=kb.now('ac'), dep='dv', inc='dv')
        kb.step('scalar', lambda s_, o=eab[:, h * 64:(h + 1) * 64], i=kb.bufs['sd_lp1'][0:16, 0:64],
                       m=abc[0:16, h:h + 1]:
                s_.activation(o, i, AF.Exp, scale=m), dep='ac', inc='ac')
    kb.barrier('vector', kb.now('ac'))
    return dict(la=la, abc=abc, dec=dec, rbc=rbc, lmt=lmt, eab=eab)


def emit_ssd_a(kb, x, xln, stats):
    nc = kb.nc
    C = emit_ssd_consts(kb)
    zf = kb.sbuf('sd_z', [128, 16 * 515], BF16)
    xb1 = kb.sbuf('sd_x1', [128, 17 * 515], BF16)
    xb2 = kb.sbuf('sd_x2', [128, 18 * 515], BF16)
    cfm0 = kb.sbuf('sd_c0', [16, 515], BF16)
    tmb = kb.sbuf('sd_tm', [128, 18 * 128], BF16)
    wsb = kb.sbuf('sd_ws', [128, 8 * 64], BF16)
    bd = kb.sbuf('sd_bd', [128, 8 * 16], BF16)
    carry = stats[0:16, 0:2048]
    sst = stats[0:16, 2052:4100]
    ydg = None
    cw = kb.bufs['sd_cw']
    cb = kb.bufs['sd_cb']
    ident = kb.bufs['ident_sb']
    ps0, ps1 = kb.bufs['ps0'], kb.bufs['ps1']

    ydg = zf[:, 0:2048]
    kb.step('vector', lambda v, o=carry[:, :]: v.memset(o, 0.0), inc='dv')
    kb.step('vector', lambda v, o=xb1[:, :]: v.memset(o, 0.0), inc='dv')
    kb.step('vector', lambda v, o=xb2[:, :]: v.memset(o, 0.0), inc='dv')

    for ti in range(4):
        t0 = HALO + ti * 512
        tb = [(t0 - 3, 512), (t0 + 509, 3)]
        def ep(o, blk, ps_ap, waits):
            off, w = blk
            c0 = off - (t0 - 3)
            if o < 16:
                kb.step('scalar', lambda s_, o_=zf[:, o * 515 + c0:o * 515 + c0 + w], i=ps_ap:
                        s_.activation(o_, i, AF.Silu), waits=waits, inc='ac')
            elif o < 32:
                kb.step('vector', lambda v, o_=xb1[:, (o - 16) * 515 + c0:(o - 16) * 515 + c0 + w],
                               i=ps_ap:
                        v.tensor_copy(o_, i), waits=waits, dep='dv', inc='dv')
            else:
                kb.step('vector', lambda v, o_=xb1[0:40, 16 * 515 + c0:16 * 515 + c0 + w],
                               i=ps_ap[0:40, :], m=cb[0:40, 17:18]:
                        v.tensor_scalar(o_, i, m, None, ALU.add), waits=waits, dep='dv',
                        inc='dv')
        ep.engine = 'vector'
        emit_proj(kb, 'sdp%d' % ti, kb.dram['ssd_WinT'], DM, 4136,
                  lambda k, off, w: xln[:, k * TH + off: k * TH + off + w], ep, tblocks=tb)
        prj = kb.now('dv', 'ac')
        for d in range(17):
            rows = 128 if d < 16 else 32
            src = lambda j, d=d, rows=rows: xb1[0:rows, d * 515 + j:d * 515 + j + 512]
            o_ = xb2[0:rows, d * 515 + 3:d * 515 + 3 + 512]
            kb.step('vector', lambda v, o2=o_, i=src(0), m=cw[0:rows, d * 4:d * 4 + 1]:
                    v.tensor_scalar(o2, i, m, None, ALU.mult), waits=prj if d == 0 else [],
                    dep='dv', inc='dv')
            for j in range(1, 4):
                kb.step('vector', lambda v, o2=o_, i=src(j),
                               m=cw[0:rows, d * 4 + j:d * 4 + j + 1], p2=o_:
                        v.scalar_tensor_tensor(o2, i, m, p2, ALU.mult, ALU.add),
                        dep='dv', inc='dv')
            kb.step('scalar', lambda s_, o2=o_, i=o_, m=cb[0:rows, d:d + 1]:
                    s_.activation(o2, i, AF.Silu, bias=m), waits=kb.now('dv'), inc='ac')
        kb.step('sync', lambda e, o=xb2[0:8, 17 * 515:17 * 515 + 515],
                       i=xb1[32:40, 16 * 515:16 * 515 + 515]:
                e.dma_start(out=o, in_=i), waits=kb.now('dv', 'ac'), inc='ld6', inc_n=16)
        kb.step('sync', lambda e, o=cfm0[0:16, :], i=xb2[16:32, 16 * 515:16 * 515 + 515]:
                e.dma_start(out=o, in_=i), inc='ld6', inc_n=16)
        kb.step('sync', lambda e,
                       o=kb.dram['ssd_zfm'].rearrange("(kc p) t -> p kc t", p=128)[:, :, ti * 512:(ti + 1) * 512],
                       i=zf.ap().rearrange("p (kc c) -> p kc c", kc=16)[:, :, 3:515]:
                e.dma_start(out=o, in_=i), inc='ld6', inc_n=16)
        ld6 = kb.val('ld6')
        kb.barrier('tensor', kb.now('dv', 'ac') + [('ld6', ld6)])
        for st in range(4):
            for d in range(18):
                pslot = d % 2
                pv = ps1.ap().bitcast(BF16)[:, 2048 + pslot * 1024:2048 + pslot * 1024 + 128]
                kb.step('tensor', lambda t, p=pv,
                               i=xb2[:, d * 515 + 3 + st * 128:d * 515 + 3 + (st + 1) * 128]:
                        t.transpose(p, i, ident[:, :]),
                        waits=kb.now('dv') if d >= 2 or st > 0 else [], inc='pe')
                pe_t = kb.val('pe')
                kb.step('vector', lambda v, o=tmb[:, d * 128:(d + 1) * 128], i=pv:
                        v.tensor_copy(o, i), waits=[('pe', pe_t)], dep='dv', inc='dv')
            TM = tmb[:, 0:18 * 128]

            dtc = TM[:, 17 * 128:17 * 128 + 8]
            sp1 = kb.bufs['scr_f32'][:, 0:8]
            sp2 = kb.bufs['scr_f32'][:, 8:16]
            kb.step('scalar', lambda s_, o=sp1, i=dtc: s_.activation(o, i, AF.Abs),
                    waits=kb.now('dv'), inc='ac')
            kb.step('scalar', lambda s_, o=sp1, i=sp1: s_.activation(o, i, AF.Exp, scale=-1.0),
                    dep='ac', inc='ac')
            kb.step('scalar', lambda s_, o=sp1, i=sp1: s_.activation(o, i, AF.Ln, bias=1.0),
                    dep='ac', inc='ac')
            kb.step('vector', lambda v, o=sp2, i=dtc: v.tensor_scalar(o, i, 0.0, None, ALU.max),
                    waits=kb.now('ac'), dep='dv', inc='dv')
            kb.step('vector', lambda v, o=sp2, a=sp2, b=sp1: v.tensor_tensor(o, a, b, ALU.add),
                    dep='dv', inc='dv')
            for h in range(SSD_H):
                kb.step('vector', lambda v, o=TM[:, h * 256:(h + 1) * 256],
                               i=TM[:, h * 256:(h + 1) * 256], m=sp2[:, h:h + 1]:
                        v.tensor_scalar(o, i, m, None, ALU.mult), dep='dv', inc='dv')
            xh_done = kb.val('dv')
            for h in range(SSD_H):
                kb.step('vector', lambda v, o=bd[:, h * 16:(h + 1) * 16],
                               i=TM[:, 16 * 128:16 * 128 + 16], m=C['dec'][:, h:h + 1]:
                        v.tensor_scalar(o, i, m, None, ALU.mult), dep='dv', inc='dv')
            bd_done = kb.val('dv')
            for ci in range(2):
                c_abs = ti * 8 + st * 2 + ci
                cb_ = ci * 64
                for h in range(SSD_H):
                    kb.step('tensor', lambda t, p=ps1[0:16, h * 256:(h + 1) * 256],
                                   l=bd[cb_:cb_ + 64, h * 16:(h + 1) * 16],
                                   r_=TM[cb_:cb_ + 64, h * 256:(h + 1) * 256]:
                            t.matmul(p, l, r_, start=True, stop=True, skip_group_check=True),
                            waits=[('dv', bd_done)] if h == 0 else [],
                            inc='pe' if h == SSD_H - 1 else None)
                kb.step('tensor', lambda t, p=ps1[64:128, 0:64],
                               l=xb2[0:16, 16 * 515 + 3 + st * 128 + cb_:
                                     16 * 515 + 3 + st * 128 + cb_ + 64],
                               r_=cfm0[0:16, 3 + st * 128 + cb_:3 + st * 128 + cb_ + 64]:
                        t.matmul(p, l, r_, start=True, stop=True, skip_group_check=True),
                        inc='pe')
                pe_st = kb.val('pe')
                kb.step('vector', lambda v, o=sst[:, :], i=ps1[0:16, 0:2048]:
                        v.tensor_copy(o, i), waits=[('pe', pe_st)] + kb.now('st4'),
                        dep='dv', inc='dv')
                kb.step('vector', lambda v, o=carry[:, :], a=carry[:, :], b=C['rbc'][:, :]:
                        v.tensor_tensor(o, a, b, ALU.mult), dep='dv', inc='dv')
                kb.step('vector', lambda v, o=carry[:, :], a=carry[:, :], b=sst[:, :]:
                        v.tensor_tensor(o, a, b, ALU.add), dep='dv', inc='dv')
                kb.step('sync', lambda e, o=kb.dram['ssd_sst'][c_abs, :, :], i=sst[:, :]:
                        e.dma_start(out=o, in_=i), dep='dv', inc='st4', inc_n=16)
                for h in range(SSD_H):
                    kb.step('vector', lambda v, o=wsb[cb_:cb_ + 64, h * 64:(h + 1) * 64],
                                   a=ps1[64:128, 0:64],
                                   b=C['lmt'][cb_:cb_ + 64, h * 64:(h + 1) * 64]:
                            v.tensor_tensor(o, a, b, ALU.mult),
                            waits=[('pe', pe_st)] if h == 0 else [], dep='dv', inc='dv')
                w_done = kb.val('dv')
                for h in range(SSD_H):
                    kb.step('tensor', lambda t, p=ps0[cb_:cb_ + 64, h * 256:(h + 1) * 256],
                                   l=wsb[cb_:cb_ + 64, h * 64:(h + 1) * 64],
                                   r_=TM[cb_:cb_ + 64, h * 256:(h + 1) * 256]:
                            t.matmul(p, l, r_, start=True, stop=True, skip_group_check=True),
                            waits=[('dv', w_done)] if h == 0 else [],
                            inc='pe' if h == SSD_H - 1 else None)
                kb.step('vector', lambda v, o=ydg[cb_:cb_ + 64, :],
                               i=ps0[cb_:cb_ + 64, 0:2048]:
                        v.tensor_copy(o, i), waits=kb.now('pe'), dep='dv', inc='dv')
            kb.step('sync', lambda e, o=kb.dram['ssd_xh'][ti * 512 + st * 128:
                           ti * 512 + (st + 1) * 128, :], i=TM[:, 0:2048]:
                    e.dma_start(out=o, in_=i), waits=kb.now('dv'), inc='st4', inc_n=16)
            kb.step('sync', lambda e, o=kb.dram['ssd_yd'][ti * 512 + st * 128:
                           ti * 512 + (st + 1) * 128, :], i=ydg[:, :]:
                    e.dma_start(out=o, in_=i), inc='st4', inc_n=16)
            kb.step('sync', lambda e, o=kb.dram['ssd_cfm'][:, ti * 512 + st * 128:
                           ti * 512 + (st + 1) * 128],
                           i=cfm0[0:16, 3 + st * 128:3 + (st + 1) * 128]:
                    e.dma_start(out=o, in_=i), inc='st4', inc_n=16)
            kb.barrier('tensor', [('st4', kb.val('st4'))])
    kb.step('sync', lambda e, o=kb.dram['ssd_G'], i=carry[:, :]:
            e.dma_start(out=o, in_=i), waits=kb.now('dv'), inc='st', inc_n=16)



def emit_ssd_b(kb, x, xln, stats):
    nc = kb.nc
    ybuf = kb.sbuf('sb_y', [128, 2048], F32)
    ysq = kb.sbuf('sb_ysq', [128, 2048], F32)
    C = emit_ssd_consts(kb, scr=ysq)
    carry = kb.sbuf('sb_cy', [16, 2048], F32)
    sstb = [kb.sbuf('sb_ss%d' % i, [16, 2048], F32) for i in range(2)]
    cfm = kb.sbuf('sb_cf', [16, T], BF16)
    ceb = kb.sbuf('sb_ce', [16, 8 * 64], F32)
    mb = kb.sbuf('sb_mb', [16, 8], F32)
    ydg = [kb.sbuf('sb_yd%d' % i, [128, 2048], BF16) for i in range(2)]
    xhb = [kb.sbuf('sb_xh%d' % i, [128, 2048], BF16) for i in range(2)]
    zsb = [kb.sbuf('sb_zs%d' % i, [128, 16 * 128], BF16) for i in range(2)]
    ztm_st = kb.sbuf('sb_ztm', [128, 2048], BF16)
    y3b = ztm_st
    yfm = kb.sbuf('sb_yf', [128, 16 * 128], BF16)
    rcol = kb.sbuf('sb_rc', [128, 4], F32)
    dbc = kb.sbuf('sb_db', [128, 8], F32)
    nwt = kb.sbuf('sb_nw', [128, 2048], BF16)
    wot = kb.sbuf('sb_wo', [128, 16 * 1024], BF16)
    ident = kb.bufs['ident_sb']
    ps0, ps1 = kb.bufs['ps0'], kb.bufs['ps1']
    tmp16 = ysq[0:16, 0:2048]
    lr = ybuf[0:16, 0:2048]

    kb.dma('sync', cfm[:, :], kb.dram['ssd_cfm_in'], inc='ld7')
    kb.dma('sync', mb[0:16, 0:8], kb.dram['ssd_coef'], inc='ld7')
    kb.dma('gpsimd', nwt[:, :], kb.dram['ssd_nw'], inc='ldg7')
    kb.dma('gpsimd', wot.ap().rearrange("p (kc o) -> p kc o", kc=16),
           kb.dram['ssd_WoutT'].rearrange("(kc p) o -> p kc o", p=128), inc='ldg7')
    kb.dma('sync', dbc[0:128, 0:8], kb.dram['ssd_dbc'], inc='ld7')
    ld7 = kb.val('ld7')
    kb.step('vector', lambda v, o=lr[:, :], i=C['rbc'][:, :]:
            v.tensor_scalar(o, i, 1e-30, None, ALU.max),
            waits=[('ld7', ld7)] + kb.now('dv', 'ac'), dep='dv', inc='dv')
    kb.step('scalar', lambda s_, o=lr[:, :], i=lr[:, :]:
            s_.activation(o, i, AF.Ln), waits=kb.now('dv'), inc='ac')
    kb.step('vector', lambda v, o=carry[:, :]: v.memset(o, 0.0), dep='dv', inc='dv')
    for j in range(4):
        kb.dma('sync', sstb[0][:, :], kb.dram['ssd_Gin'][j * 16:(j + 1) * 16, :],
               waits=kb.now('dv'), inc='ld7')
        kb.step('scalar', lambda s_, o=tmp16[:, :], i=lr[:, :], m=mb[0:16, j:j + 1]:
                s_.activation(o, i, AF.Exp, scale=m), waits=kb.now('dv'), dep='ac', inc='ac')
        kb.step('vector', lambda v, o=tmp16[:, :], i=tmp16[:, :], m=mb[0:16, 4 + j:5 + j]:
                v.tensor_scalar(o, i, m, None, ALU.mult), waits=kb.now('ac'), dep='dv',
                inc='dv')
        kb.step('vector', lambda v, o=tmp16[:, :], a=tmp16[:, :], b=sstb[0][:, :]:
                v.tensor_tensor(o, a, b, ALU.mult), waits=[('ld7', kb.val('ld7'))],
                dep='dv', inc='dv')
        kb.step('vector', lambda v, o=carry[:, :], a=carry[:, :], b=tmp16[:, :]:
                v.tensor_tensor(o, a, b, ALU.add), dep='dv', inc='dv')

    for sp in range(16):
        c0 = sp * 2
        for (buf, key) in ((ydg[sp % 2], 'ssd_yd_in'), (xhb[sp % 2], 'ssd_xh_in')):
            kb.dma('sync', buf[:, :],
                   kb.dram[key][sp * 128:(sp + 1) * 128, :],
                   waits=kb.now('dv') if sp >= 2 else [], inc='ld8%d' % (sp % 2))
        kb.dma('sync', zsb[sp % 2].ap().rearrange("p (kc c) -> p kc c", kc=16),
               kb.dram['ssd_zfm_in'].rearrange("(kc p) t -> p kc t", p=128)[:, :, sp * 128:(sp + 1) * 128],
               waits=kb.now('dv') if sp >= 2 else [], inc='ld8%d' % (sp % 2))
        for ci in range(2):
            kb.dma('sync', sstb[ci][:, :], kb.dram['ssd_sst_in'][c0 + ci, :, :],
                   waits=kb.now('dv'), inc='ld8%d' % (sp % 2))
        ld8 = kb.val('ld8%d' % (sp % 2))
        for ci in range(2):
            c = c0 + ci
            cb_ = ci * 64
            for h in range(SSD_H):
                kb.step('vector', lambda v, o=ceb[:, h * 64:(h + 1) * 64],
                               a=cfm[:, c * 64:(c + 1) * 64], b=C['eab'][:, h * 64:(h + 1) * 64]:
                        v.tensor_tensor(o, a, b, ALU.mult), dep='dv', inc='dv')
            ce_done = kb.val('dv')
            for h in range(SSD_H):
                kb.step('tensor', lambda t, p=ps0[cb_:cb_ + 64, h * 256:(h + 1) * 256],
                               l=ceb[:, h * 64:(h + 1) * 64],
                               r_=carry[:, h * 256:(h + 1) * 256]:
                        t.matmul(p, l, r_, start=True, stop=True, skip_group_check=True),
                        waits=[('dv', ce_done)] if h == 0 else [],
                        inc='pe' if h == SSD_H - 1 else None)
            pe_y = kb.val('pe')
            kb.step('vector', lambda v, o=carry[:, :], a=carry[:, :], b=C['rbc'][:, :]:
                    v.tensor_tensor(o, a, b, ALU.mult), waits=[('pe', pe_y)], dep='dv',
                    inc='dv')
            kb.step('vector', lambda v, o=carry[:, :], a=carry[:, :], b=sstb[ci][:, :]:
                    v.tensor_tensor(o, a, b, ALU.add),
                    waits=[('ld8%d' % (sp % 2), ld8)], dep='dv', inc='dv')
        kb.step('vector', lambda v, o=ybuf[:, :], a=ydg[sp % 2][:, :], b=ps0[:, 0:2048]:
                v.tensor_tensor(o, a, b, ALU.add),
                waits=kb.now('pe') + [('ld8%d' % (sp % 2), ld8)], dep='dv', inc='dv')
        for h in range(SSD_H):
            kb.step('vector', lambda v, o=ybuf[:, h * 256:(h + 1) * 256],
                           i=xhb[sp % 2][:, h * 256:(h + 1) * 256], m=dbc[:, h:h + 1],
                           p2=ybuf[:, h * 256:(h + 1) * 256]:
                    v.scalar_tensor_tensor(o, i, m, p2, ALU.mult, ALU.add), dep='dv', inc='dv')
        for k in range(16):
            pslot = k % 2
            pv = ps1.ap().bitcast(BF16)[:, 2048 + pslot * 1024:2048 + pslot * 1024 + 128]
            kb.step('tensor', lambda t, p=pv, i=zsb[sp % 2][:, k * 128:(k + 1) * 128]:
                    t.transpose(p, i, ident[:, :]),
                    waits=([('ld8%d' % (sp % 2), ld8)] if k == 0 else []) +
                          (kb.now('dv') if k >= 2 else []), inc='pe')
            pe_t = kb.val('pe')
            kb.step('vector', lambda v, o=ztm_st[:, k * 128:(k + 1) * 128], i=pv:
                    v.tensor_copy(o, i), waits=[('pe', pe_t)], dep='dv', inc='dv')
        kb.step('vector', lambda v, o=ybuf[:, :], a=ybuf[:, :], b=ztm_st[:, :]:
                v.tensor_tensor(o, a, b, ALU.mult), dep='dv', inc='dv')
        kb.step('scalar', lambda s_, o=ysq[:, :], i=ybuf[:, :]:
                s_.activation(o, i, AF.Square), waits=kb.now('dv'), inc='ac')
        kb.step('vector', lambda v, o=rcol[:, 0:1], i=ysq[:, :]:
                v.reduce_sum(o, i, AX.X), waits=kb.now('ac'), dep='dv', inc='dv')
        kb.step('vector', lambda v, o=rcol[:, 1:2], i=rcol[:, 0:1]:
                v.tensor_scalar(o, i, 1.0 / 2048, SSD_EPS, ALU.mult, ALU.add),
                dep='dv', inc='dv')
        kb.step('scalar', lambda s_, o=rcol[:, 1:2], i=rcol[:, 1:2]:
                s_.activation(o, i, AF.Ln), waits=kb.now('dv'), inc='ac')
        kb.step('scalar', lambda s_, o=rcol[:, 2:3], i=rcol[:, 1:2]:
                s_.activation(o, i, AF.Exp, scale=-0.5), dep='ac', inc='ac')
        kb.step('vector', lambda v, o=ybuf[:, :], i=ybuf[:, :], m=rcol[:, 2:3]:
                v.tensor_scalar(o, i, m, None, ALU.mult), waits=kb.now('ac'), dep='dv',
                inc='dv')
        kb.step('vector', lambda v, o=y3b[:, :], a=ybuf[:, :], b=nwt[:, :]:
                v.tensor_tensor(o, a, b, ALU.mult),
                waits=[('ldg7', kb.val('ldg7'))], dep='dv', inc='dv')
        y3_done = kb.val('dv')
        for k in range(16):
            pslot = k % 2
            pv = ps1.ap().bitcast(BF16)[:, 2048 + pslot * 1024:2048 + pslot * 1024 + 128]
            kb.step('tensor', lambda t, p=pv, i=y3b[:, k * 128:(k + 1) * 128]:
                    t.transpose(p, i, ident[:, :]),
                    waits=([('dv', y3_done)] if k == 0 else []) +
                          (kb.now('dv') if k >= 2 else []), inc='pe')
            pe_t = kb.val('pe')
            kb.step('vector', lambda v, o=yfm[:, k * 128:(k + 1) * 128], i=pv:
                    v.tensor_copy(o, i), waits=[('pe', pe_t)], dep='dv', inc='dv')
        yfm_done = kb.val('dv')
        for o in range(8):
            for k in range(16):
                kb.step('tensor', lambda t, p=ps1[:, o * 128:(o + 1) * 128],
                               l=wot[:, k * 1024 + o * 128:k * 1024 + (o + 1) * 128],
                               r_=yfm[:, k * 128:(k + 1) * 128], f=(k == 0), lst=(k == 15):
                        t.matmul(p, l, r_, start=f, stop=lst, skip_group_check=True),
                        waits=([('dv', yfm_done), ('ldg7', kb.val('ldg7'))]
                               if (o == 0 and k == 0) else []),
                        inc='pe' if (k == 15) else None)
        pe_p = kb.val('pe')
        for o in range(8):
            xs = x[:, o * TH + HALO + sp * 128:o * TH + HALO + (sp + 1) * 128]
            kb.step('vector', lambda v, o_=xs, a=xs, b=ps1[:, o * 128:(o + 1) * 128]:
                    v.tensor_tensor(o_, a, b, ALU.add),
                    waits=[('pe', pe_p)] if o == 0 else [], dep='dv', inc='dv')



def fm_in(ap):
    return ap.rearrange("(kc p) t -> p kc t", p=128)


def build_launch(sub_emitters, extra_inputs=(), extra_outputs=(), weights=(),
                 want_mu=False, want_fsc=False, pw_cols=8 * 128, preload=(),
                 want_xln=True, want_scr=True):
    nc = bass.Bass("TRN2", target_bir_lowering=False, debug=False)
    dram = {}
    dram['x'] = nc.dram_tensor('x', [DM, TH], F32, kind="ExternalInput").ap()
    dram['x_out'] = nc.dram_tensor('x_out', [DM, T], F32, kind="ExternalOutput").ap()
    dram['ones_col'] = nc.dram_tensor('ones_col', [128, 1], F32, kind="ExternalInput").ap()
    dram['ones_row'] = nc.dram_tensor('ones_row', [1, 128], F32, kind="ExternalInput").ap()
    if want_mu:
        dram['mu_rk'] = nc.dram_tensor('mu_rk', [128, 16], F32, kind="ExternalInput").ap()
    if want_fsc:
        dram['fsc'] = nc.dram_tensor('fsc', [128, 136], F32, kind="ExternalInput").ap()
    for nm, shape, dt in extra_inputs:
        dram[nm] = nc.dram_tensor(nm, shape, dt, kind="ExternalInput").ap()
    for nm, shape, dt in extra_outputs:
        dram[nm] = nc.dram_tensor(nm, shape, dt, kind="ExternalOutput").ap()
    for nm, shape in weights:
        dram[nm] = nc.dram_tensor(nm, list(shape), F32, kind="ExternalInput").ap()
    for nm, shape, dt_, dk, dshape, ddt in preload:
        if dk not in dram:
            dram[dk] = nc.dram_tensor(dk, list(dshape), ddt, kind="ExternalInput").ap()

    with contextlib.ExitStack() as stk:
        kb = KB(nc, stk)
        kb.dram = dram
        kb.pw_cols = pw_cols
        x = kb.sbuf('x_sb', [128, DCH * TH], F32)
        xln = kb.sbuf('xln_sb', [128, DCH * TH], BF16) if want_xln else None
        stats = kb.sbuf('stats_sb', [128, 2 * TH], F32) if want_xln else None
        if want_scr:
            kb.sbuf('scr_f32', [128, 1536], F32)
        kb.psum('ps0', [128, 2048])
        kb.psum('ps1', [128, 2048])
        oc = kb.sbuf('ones_col_sb', [128, 1], F32)
        kb.bufs['ones_col'] = oc
        orow = kb.sbuf('ones_row_sb', [1, 128], F32)
        kb.bufs['ones_row'] = orow
        kb.dma('sync', oc[:, :], dram['ones_col'])
        kb.dma('sync', orow[:, :], dram['ones_row'])
        if want_mu:
            mu = kb.sbuf('mu_sb', [128, 32], F32)
            kb.dma('sync', mu[:, 0:16], dram['mu_rk'])
        if want_fsc:
            fsc = kb.sbuf('fsc_sb', [128, 136], F32)
            kb.dma('sync', fsc[:, :], dram['fsc'])
        for nm, shape, dt_, dk, dshape, ddt in preload:
            b = kb.sbuf(nm, list(shape), dt_)
            if dt_ != ddt:
                kb.dma('gpsimd', b[0:shape[0], :], dram[dk], inc='ldg')
            else:
                kb.dma('sync', b[0:shape[0], :], dram[dk])
        kb.dma('sync', x.ap().rearrange("p (kc t) -> p kc t", kc=DCH), fm_in(dram['x']))
        ocb = kb.sbuf('ones_col_bf', [128, 1], BF16)
        ldx = kb.val('ld')
        ldg = kb.val('ldg') if 'ldg' in kb.plan else 0
        for e in ('tensor', 'vector', 'scalar'):
            kb.barrier(e, [('ld', ldx)] + ([('ldg', ldg)] if ldg else []))
        kb.step('vector', lambda v, o=ocb[:, :], i=oc[:, :]: v.tensor_copy(o, i), inc='dv')

        for em in sub_emitters:
            em(kb, x, xln, stats)

        out3 = dram['x_out'].rearrange("(kc p) t -> p kc t", p=128)
        in3 = x.ap().rearrange("p (kc th) -> p kc th", kc=DCH)[:, :, HALO:HALO + T]
        kb.step('sync', lambda e: e.dma_start(out=out3, in_=in3),
                waits=kb.now('dv', 'ac', 'pe'), inc='st', inc_n=16)
        kb.step('sync', lambda e: e.nop(), waits=kb.now('st'))

        with nc.Block() as block:
            kb.finalize(block)
    return nc


def to_fm(a):
    return np.ascontiguousarray(a.T)


ONES_COL = np.ones((128, 1), np.float32)
ONES_ROW = np.ones((1, 128), np.float32)


IDENT_NP = np.eye(128, dtype=np.float32)
J128_NP = np.ascontiguousarray(np.fliplr(np.eye(128))).astype(np.float32)


def _chunk_major(vec, nch):
    return np.ascontiguousarray(np.asarray(vec, np.float32).reshape(nch, 128).T)


def _shard_x_arrays(xfull):
    shards = []
    for c in range(NCORES):
        b, q = c // 4, c % 4
        s0 = q * T
        tile = np.zeros((TH, DM), np.float32)
        tile[HALO:] = xfull[b, s0:s0 + T]
        if s0 > 0:
            tile[:HALO] = xfull[b, s0 - HALO:s0]
        shards.append(np.ascontiguousarray(tile.T))
    return shards


def _reshard(outs):
    xs = [np.asarray(o['x_out']) for o in outs]
    res = []
    for c in range(NCORES):
        b, q = c // 4, c % 4
        tile = np.zeros((DM, TH), np.float32)
        tile[:, HALO:] = xs[c]
        if q > 0:
            tile[:, :HALO] = xs[c - 1][:, T - HALO:]
        res.append(tile)
    return res


_PROGS = {}


def _prog(name, builder):
    if name not in _PROGS:
        _PROGS[name] = builder()
    return _PROGS[name]


def kernel_unfused(**inp):
    from concourse.bass_utils import run_bass_kernel_spmd
    d = inp
    cores = list(range(NCORES))

    base = dict(ones_col=ONES_COL, ones_row=ONES_ROW)

    def run(nc, maps):
        return run_bass_kernel_spmd(nc, maps, core_ids=cores).results

    xsh = _shard_x_arrays(d['L0_attn_a_x'].astype(np.float32))

    nc_a = _prog('l0a_qkv', lambda: build_launch(
        [lambda kb, x, xln, stats: (emit_ln(kb, x, xln, stats),
                                    emit_l0a_qkv(kb, x, xln, stats),
                                    kb.step('sync', lambda e: e.dma_start(
                                        out=kb.dram['l0a_qloc'], in_=kb.bufs['a0_q'][:, :]),
                                        waits=kb.now('dv', 'ac'), inc='st', inc_n=16))],
        extra_outputs=[('l0a_qloc', [64, 8 * T], BF16), ('l0a_Kfm_loc', [512, T], BF16),
                       ('l0a_Vtm_loc', [T, 512], BF16), ('l0a_nrm_loc', [1, 32], F32)],
        weights=[('l0a_WqT', (1024, 512)), ('l0a_Wkv', (1024, 1024))],
        preload=[('ident_sb', [128, 128], BF16, 'ident', [128, 128], F32)]))
    maps = [dict(x=xs, ident=IDENT_NP, l0a_WqT=d['L0_attn_a_Wq'],
                 l0a_Wkv=d['L0_attn_a_Wkv'], **base) for xs in xsh]
    outs = run(nc_a, maps)

    nc_b = _prog('l0a_attn', lambda: build_launch(
        [lambda kb, x, xln, stats: emit_l0a_attn(kb, x, xln, stats)], want_xln=False,
        extra_inputs=[('l0a_Kfm', [512, SEQ], BF16), ('l0a_Vtm', [SEQ, 512], BF16),
                      ('l0a_nrm_in', [1, 128], F32), ('l0a_qn', [1, 16], F32),
                      ('l0a_lam', [1, 128], F32), ('l0a_gammaT', [64, 1], F32)],
        weights=[('l0a_WoutT', (512, 1024))],
        preload=[('a0_q', [64, 8 * T], BF16, 'l0a_qloc', [64, 8 * T], BF16)]))
    lamvec = np.concatenate([d['L0_attn_a_lambda_q1'], d['L0_attn_a_lambda_k1'],
                             d['L0_attn_a_lambda_q2'], d['L0_attn_a_lambda_k2']]
                            )[None, :].astype(np.float32)
    gT = np.ascontiguousarray(np.asarray(d['L0_attn_a_gamma'], np.float32).reshape(64, 1))
    maps = []
    for c in range(NCORES):
        b, q = c // 4, c % 4
        grp = range(b * 4, b * 4 + 4)
        maps.append(dict(
            x=xsh[c], l0a_Kfm=np.concatenate([np.asarray(outs[g]['l0a_Kfm_loc']) for g in grp], 1),
            l0a_Vtm=np.concatenate([np.asarray(outs[g]['l0a_Vtm_loc']) for g in grp], 0),
            l0a_nrm_in=np.concatenate([np.asarray(outs[g]['l0a_nrm_loc']) for g in grp], 1),
            l0a_qn=np.asarray(outs[c]['l0a_nrm_loc'])[:, 0:16], l0a_lam=lamvec,
            l0a_gammaT=gT, l0a_WoutT=d['L0_attn_a_Wout'],
            l0a_qloc=np.asarray(outs[c]['l0a_qloc']), **base))
    outs = run(nc_b, maps)
    xsh = _reshard(outs)

    nc_f = _prog('ffn0', lambda: build_launch(
        [lambda kb, x, xln, stats: (emit_ln(kb, x, xln, stats),
                                    emit_ffn0(kb, x, xln, 'w1T', 'w2T', 'h1T', 'h2T', 'w3T'))],
        weights=[('w1T', (1024, 4096)), ('w2T', (1024, 4096)), ('h1T', (4096, 4096)),
                 ('h2T', (4096, 4096)), ('w3T', (4096, 1024))],
        want_fsc=True, pw_cols=32 * 128))
    fsc = np.zeros((128, 136), np.float32)
    for i, k in enumerate(['w1_s', 'w2_s', 'h1_s', 'h2_s']):
        fsc[:, i * 32:(i + 1) * 32] = np.asarray(d['L0_ffn_' + k], np.float32).reshape(32, 128).T
    fsc[:, 128:136] = np.asarray(d['L0_ffn_w3_s'], np.float32).reshape(8, 128).T
    wsf = dict(w1T=np.ascontiguousarray(d['L0_ffn_w1_w'].T),
               w2T=np.ascontiguousarray(d['L0_ffn_w2_w'].T),
               h1T=np.ascontiguousarray(d['L0_ffn_h1_w'].T),
               h2T=np.ascontiguousarray(d['L0_ffn_h2_w'].T),
               w3T=np.ascontiguousarray(d['L0_ffn_w3_w'].T))
    maps = [dict(x=xs, fsc=fsc, **wsf, **base) for xs in xsh]
    outs = run(nc_f, maps)
    xsh = _reshard(outs)

    consts_in = [('ssd_logA', [1, 8], F32), ('sd_pos_d', [128, 1], F32),
                 ('sd_ldif_d', [128, 64], F32), ('sd_mask_d', [128, 64], F32),
                 ('sd_lp1_d', [16, 64], F32), ('sd_sel8_d', [8, 2048], F32)]
    consts_pre = [('ident_sb', [128, 128], BF16, 'ident', [128, 128], F32),
                  ('sd_pos', [128, 1], F32, 'sd_pos_d', [128, 1], F32),
                  ('sd_ldif', [128, 64], BF16, 'sd_ldif_d', [128, 64], F32),
                  ('sd_mask', [128, 64], BF16, 'sd_mask_d', [128, 64], F32),
                  ('sd_lp1', [16, 64], BF16, 'sd_lp1_d', [16, 64], F32)]
    nc_sa = _prog('ssd_a', lambda: build_launch(
        [lambda kb, x, xln, stats: (emit_ln(kb, x, xln, stats), emit_ssd_a(kb, x, xln, stats))],
        extra_inputs=consts_in,
        extra_outputs=[('ssd_xh', [T, 2048], BF16), ('ssd_yd', [T, 2048], BF16),
                       ('ssd_zfm', [2048, T], BF16), ('ssd_cfm', [16, T], BF16),
                       ('ssd_sst', [32, 16, 2048], F32), ('ssd_G', [16, 2048], F32)],
        weights=[('ssd_WinT', (DM, 4136))],
        preload=consts_pre + [('sd_cw', [128, 17 * 4], F32, 'sd_cw_d', [128, 17 * 4], F32),
                              ('sd_cb', [128, 18], F32, 'sd_cb_d', [128, 18], F32)]))
    p_ = np.arange(128)
    lidx = np.arange(64)
    pos = (63 - (p_ % 64)).astype(np.float32)[:, None]
    ldif = (lidx[None, :] - (p_ % 64)[:, None]).astype(np.float32)
    ldif[lidx[None, :] < (p_ % 64)[:, None].repeat(64, 1)] = 30000.0
    maskc = (lidx[None, :] >= (p_ % 64)[:, None]).astype(np.float32)
    lp1 = np.broadcast_to((lidx + 1).astype(np.float32)[None, :], (16, 64)).copy()
    cw = np.zeros((128, 17 * 4), np.float32)
    cbv = np.zeros((128, 18), np.float32)
    for dd in range(17):
        nch = 128 if dd < 16 else 32
        cw[0:nch, dd * 4:(dd + 1) * 4] = d['L0_attn_b_conv_w'][dd * 128:dd * 128 + nch, 0, :]
        cbv[0:nch, dd] = d['L0_attn_b_conv_b'][dd * 128:dd * 128 + nch]
    cbv[32:40, 17] = d['L0_attn_b_dt_bias']
    sel8 = np.zeros((8, 2048), np.float32)
    for hh in range(8):
        sel8[hh, hh * 256:(hh + 1) * 256] = 1.0
    ssd_base = dict(ssd_logA=np.asarray(d['L0_attn_b_log_A'], np.float32)[None, :],
                    sd_pos_d=pos, sd_ldif_d=ldif, sd_mask_d=maskc, sd_lp1_d=lp1,
                    sd_sel8_d=sel8, ident=IDENT_NP, **base)
    winT = np.ascontiguousarray(d['L0_attn_b_W_in'].T)
    maps = [dict(x=xs, sd_cw_d=cw, sd_cb_d=cbv, ssd_WinT=winT, **ssd_base) for xs in xsh]
    outs = run(nc_sa, maps)

    nc_sb = _prog('ssd_b', lambda: build_launch(
        [lambda kb, x, xln, stats: emit_ssd_b(kb, x, xln, stats)], want_xln=False,
        extra_inputs=consts_in + [
            ('ssd_Gin', [64, 2048], F32), ('ssd_coef', [16, 8], F32),
            ('ssd_nw', [128, 2048], F32), ('ssd_dbc', [128, 8], F32),
            ('ssd_yd_in', [T, 2048], BF16), ('ssd_xh_in', [T, 2048], BF16),
            ('ssd_zfm_in', [2048, T], BF16), ('ssd_cfm_in', [16, T], BF16),
            ('ssd_sst_in', [32, 16, 2048], F32)],
        weights=[('ssd_WoutT', (2048, DM))], preload=consts_pre))
    maps = []
    for c in range(NCORES):
        b, q = c // 4, c % 4
        grp = list(range(b * 4, b * 4 + 4))
        coef = np.zeros((16, 8), np.float32)
        for j in range(4):
            coef[:, j] = 32.0 * max(q - 1 - j, 0)
            coef[:, 4 + j] = 1.0 if j < q else 0.0
        maps.append(dict(
            x=xsh[c], ssd_Gin=np.concatenate([np.asarray(outs[g]['ssd_G']) for g in grp], 0),
            ssd_coef=coef,
            ssd_nw=np.broadcast_to(np.asarray(d['L0_attn_b_norm_w'], np.float32), (128, 2048)).copy(),
            ssd_dbc=np.broadcast_to(np.asarray(d['L0_attn_b_D_skip'], np.float32), (128, 8)).copy(),
            ssd_yd_in=np.asarray(outs[c]['ssd_yd']), ssd_xh_in=np.asarray(outs[c]['ssd_xh']),
            ssd_zfm_in=np.asarray(outs[c]['ssd_zfm']), ssd_cfm_in=np.asarray(outs[c]['ssd_cfm']),
            ssd_sst_in=np.asarray(outs[c]['ssd_sst']).reshape(32, 16, 2048),
            ssd_WoutT=np.ascontiguousarray(d['L0_attn_b_W_out'].T), **ssd_base))
    outs = run(nc_sb, maps)
    xsh = _reshard(outs)

    maps = [dict(x=xs, fsc=fsc, **wsf, **base) for xs in xsh]
    outs = run(nc_f, maps)
    xsh = _reshard(outs)

    nc_ya = _prog('l1a_qkv', lambda: build_launch(
        [lambda kb, x, xln, stats: (emit_ln(kb, x, xln, stats), emit_l1a_qkv(kb, x, xln, stats))],
        extra_outputs=[('l1a_qloc', [128, 8 * T], BF16), ('l1a_Kfm_loc', [DM, T], BF16),
                       ('l1a_Vtm_loc', [T, DM], BF16), ('l1a_cn', [128, 256], F32)],
        weights=[('l1a_wq', (DM, DM)), ('l1a_wk', (DM, DM)), ('l1a_wv', (DM, DM))],
        preload=[('ident_sb', [128, 128], BF16, 'ident', [128, 128], F32),
                 ('y1_bias', [128, 24], F32, 'l1a_biases', [128, 24], F32),
                 ('y1_alpha', [1, 3], F32, 'l1a_alphas', [1, 3], F32)]))
    bias_pack = np.concatenate([_chunk_major(d['L1_attn_a_bq'], 8),
                                _chunk_major(d['L1_attn_a_bk'], 8),
                                _chunk_major(d['L1_attn_a_bv'], 8)], 1)
    alphas = np.concatenate([d['L1_attn_a_aq'], d['L1_attn_a_ak'],
                             d['L1_attn_a_av']])[None, :].astype(np.float32)
    maps = [dict(x=xs, ident=IDENT_NP, l1a_biases=bias_pack, l1a_alphas=alphas,
                 l1a_wq=d['L1_attn_a_wq'], l1a_wk=d['L1_attn_a_wk'],
                 l1a_wv=d['L1_attn_a_wv'], **base) for xs in xsh]
    outs = run(nc_ya, maps)

    nc_yb = _prog('l1a_attn', lambda: build_launch(
        [lambda kb, x, xln, stats: emit_l1a_attn(kb, x, xln, stats)],
        want_xln=False, want_scr=False,
        extra_inputs=[('l1a_Kfm', [DM, SEQ], BF16), ('l1a_Vtm', [SEQ, DM], BF16),
                      ('l1a_qloc', [128, 8 * T], BF16), ('l1a_cn', [128, 256], F32)],
        weights=[('l1a_wo', (DM, DM))],
        preload=[('ident_sb', [128, 128], BF16, 'ident', [128, 128], F32),
                 ('y1_bo', [128, 8], F32, 'l1a_bo', [128, 8], F32)]))
    bo_pack = _chunk_major(d['L1_attn_a_bo'], 8)
    maps = []
    for c in range(NCORES):
        b = c // 4
        grp = range(b * 4, b * 4 + 4)
        maps.append(dict(
            x=xsh[c], ident=IDENT_NP,
            l1a_Kfm=np.concatenate([np.asarray(outs[g]['l1a_Kfm_loc']) for g in grp], 1),
            l1a_Vtm=np.concatenate([np.asarray(outs[g]['l1a_Vtm_loc']) for g in grp], 0),
            l1a_wo=d['L1_attn_a_wo'], l1a_qloc=np.asarray(outs[c]['l1a_qloc']),
            l1a_cn=np.asarray(outs[c]['l1a_cn']), l1a_bo=bo_pack, **base))
    outs = run(nc_yb, maps)
    xsh = _reshard(outs)

    nc_r = _prog('l1ffn', lambda: build_launch(
        [lambda kb, x, xln, stats: (emit_ln(kb, x, xln, stats),
                                    emit_l1ffn(kb, x, xln, 'WrT', 'WkT', 'WvT'))],
        weights=[('WrT', (DM, DM)), ('WkT', (DM, DM)), ('WvT', (DM, DM))], want_mu=True))
    mu_rk = np.ascontiguousarray(np.concatenate([_chunk_major(d['L1_ffn_mu_r'], 8),
                                                 _chunk_major(d['L1_ffn_mu_k'], 8)], 1))
    wr = dict(WrT=np.ascontiguousarray(d['L1_ffn_W_r'].T),
              WkT=np.ascontiguousarray(d['L1_ffn_W_k'].T),
              WvT=np.ascontiguousarray(d['L1_ffn_W_v'].T))
    maps = [dict(x=xs, mu_rk=mu_rk, **wr, **base) for xs in xsh]
    outs = run(nc_r, maps)
    xsh = _reshard(outs)

    nc_ba = _prog('l1b_qkv', lambda: build_launch(
        [lambda kb, x, xln, stats: (emit_ln(kb, x, xln, stats), emit_l1b_qkv(kb, x, xln, stats))],
        extra_outputs=[('l1b_qloc', [128, 8 * T], BF16), ('l1b_Kfm_loc', [DM, T], BF16),
                       ('l1b_Vtm_loc', [T, DM], BF16), ('l1b_nrm_loc', [1, 32], F32)],
        weights=[('l1b_Wq', (DM, DM)), ('l1b_Wk', (DM, DM)), ('l1b_Wv', (DM, DM))],
        preload=[('ident_sb', [128, 128], BF16, 'ident', [128, 128], F32),
                 ('b1_bias', [128, 24], F32, 'l1b_biases', [128, 24], F32)]))
    b1bias = np.concatenate([_chunk_major(d['L1_attn_b_bq'], 8),
                             _chunk_major(d['L1_attn_b_bk'], 8),
                             _chunk_major(d['L1_attn_b_bv'], 8)], 1)
    maps = [dict(x=xs, ident=IDENT_NP, l1b_biases=b1bias, l1b_Wq=d['L1_attn_b_Wq'],
                 l1b_Wk=d['L1_attn_b_Wk'], l1b_Wv=d['L1_attn_b_Wv'], **base) for xs in xsh]
    outs = run(nc_ba, maps)

    nc_bb = _prog('l1b_attn', lambda: build_launch(
        [lambda kb, x, xln, stats: emit_l1b_attn(kb, x, xln, stats)], want_xln=False,
        extra_inputs=[('l1b_Kfm', [DM, SEQ], BF16), ('l1b_Vtm', [SEQ, DM], BF16),
                      ('l1b_nrm_in', [1, 128], F32), ('l1b_qn', [1, 16], F32),
                      ('l1b_relx', [128, (16 * LPAD) // 128], F32),
                      ('l1b_relcm', [128, (16 * LPAD) // 128], F32),
                      ('l1b_rmask', [128, (16 * LPAD) // 128], F32)],
        preload=[('b1_qv', [128, 8 * T], BF16, 'l1b_qloc', [128, 8 * T], BF16),
                 ('j128_sb', [128, 128], BF16, 'j128', [128, 128], F32)]))
    re_ = np.asarray(d['L1_attn_b_rel_emb'], np.float32)

    def relx_for(q0):
        u = np.arange(LPAD)
        dd = u + U0 - q0
        valid = (dd > 4096) & (dd <= 8191)
        idx = np.clip(8191 - dd, 0, 4094)
        rx = np.where(valid[None, :], re_[:, idx], 0.0).astype(np.float32)
        mk = np.broadcast_to(valid[None, :], (16, LPAD)).astype(np.float32).copy()
        cm = np.broadcast_to(re_[:, 4094:4095], (16, LPAD)).astype(np.float32).copy()
        rs = lambda a: np.ascontiguousarray(a.reshape(128, (16 * LPAD) // 128))
        return rs(rx), rs(mk), rs(cm)
    maps = []
    for c in range(NCORES):
        b, q = c // 4, c % 4
        grp = range(b * 4, b * 4 + 4)
        rx, mk, cm = relx_for(q * T)
        maps.append(dict(
            x=xsh[c],
            l1b_Kfm=np.concatenate([np.asarray(outs[g]['l1b_Kfm_loc']) for g in grp], 1),
            l1b_Vtm=np.concatenate([np.asarray(outs[g]['l1b_Vtm_loc']) for g in grp], 0),
            l1b_nrm_in=np.concatenate([np.asarray(outs[g]['l1b_nrm_loc']) for g in grp], 1),
            l1b_qn=np.asarray(outs[c]['l1b_nrm_loc'])[:, 0:16],
            l1b_relx=rx, l1b_rmask=mk, l1b_relcm=cm,
            l1b_qloc=np.asarray(outs[c]['l1b_qloc']), j128=J128_NP, **base))
    outs = run(nc_bb, maps)
    xsh = _reshard(outs)

    maps = [dict(x=xs, mu_rk=mu_rk, **wr, **base) for xs in xsh]
    outs = run(nc_r, maps)

    xfull = np.zeros((2, SEQ, DM), np.float32)
    for c in range(NCORES):
        b, q = c // 4, c % 4
        xfull[b, q * T:(q + 1) * T] = np.asarray(outs[c]['x_out']).T
    return xfull


RG = [[0, 1, 2, 3], [4, 5, 6, 7]]


def emit_ag(kb, in_ap, out_ap, waits):
    kb.step('gpsimd', lambda g: g.collective_compute(
        'AllGather', mybir.AluOpType.bypass, replica_groups=RG,
        ins=[in_ap.opt()], outs=[out_ap.opt()]), waits=waits, inc='cc', inc_n=1)
    return kb.val('cc')


def emit_halo_exchange(kb, x):
    nc = kb.nc
    hin = kb.dram['halo_in']
    hout = kb.dram['halo_out']
    msel = kb.bufs['msel_sb']
    hsb = kb.sbuf('hx_sb', [128, 4 * DCH * HALO], F32)
    kb.step('sync', lambda e, o=hin.rearrange("(kc p) t -> p kc t", p=128),
                   i=x.ap().rearrange("p (kc th) -> p kc th", kc=DCH)[:, :, HALO + T - HALO:HALO + T]:
            e.dma_start(out=o, in_=i), waits=kb.now('dv', 'ac'), inc='hx', inc_n=16)
    ag = emit_ag(kb, hin, hout, [('hx', kb.val('hx'))])
    for r in range(4):
        kb.step('sync', lambda e, o=hsb[:, r * DCH * HALO:(r + 1) * DCH * HALO].rearrange(
                       "p (kc t) -> p kc t", kc=DCH),
                       i=hout[r * DM:(r + 1) * DM, :].rearrange("(kc p) t -> p kc t", p=128):
                e.dma_start(out=o, in_=i), waits=[('cc', ag)] if r == 0 else [],
                inc='hx', inc_n=16)
    hx = kb.val('hx')
    xh_view = x.ap().rearrange("p (kc th) -> p kc th", kc=DCH)[:, :, 0:HALO]
    kb.step('vector', lambda v, o=xh_view, i=hsb[:, 0:DCH * HALO].rearrange(
                   "p (kc t) -> p kc t", kc=DCH), m=msel[:, 0:1]:
            v.tensor_scalar(o, i, m, None, ALU.mult), waits=[('hx', hx)], dep='dv', inc='dv')
    for r in range(1, 4):
        kb.step('vector', lambda v, o=xh_view, i=hsb[:, r * DCH * HALO:(r + 1) * DCH * HALO]
                       .rearrange("p (kc t) -> p kc t", kc=DCH), m=msel[:, r:r + 1], p2=xh_view:
                v.scalar_tensor_tensor(o, i, m, p2, ALU.mult, ALU.add), dep='dv', inc='dv')


def build_fused(nphase=8):
    nc = bass.Bass("TRN2", target_bir_lowering=False, debug=False)
    dram = {}

    def din(nm, shape, dt=F32):
        dram[nm] = nc.dram_tensor(nm, list(shape), dt, kind="ExternalInput").ap()

    def dint(nm, shape, dt):
        dram[nm] = nc.dram_tensor(nm, list(shape), dt).ap()

    din('x', [DM, TH])
    dram['x_out'] = nc.dram_tensor('x_out', [DM, T], F32, kind="ExternalOutput").ap()
    din('ones_col', [128, 1]); din('ones_row', [1, 128]); din('ident', [128, 128])
    din('mu_rk', [128, 16]); din('fsc', [128, 136]); din('msel', [128, 4])
    din('j128', [128, 128])
    din('l0a_WqT', [1024, 512]); din('l0a_Wkv', [1024, 1024]); din('l0a_WoutT', [512, 1024])
    din('l0a_lam', [1, 128]); din('l0a_gammaT', [64, 1])
    for nm in ('w1T', 'w2T'):
        din(nm, [1024, 4096])
    for nm in ('h1T', 'h2T'):
        din(nm, [4096, 4096])
    din('w3T', [4096, 1024])
    din('ssd_WinT', [DM, 4136]); din('ssd_WoutT', [2048, DM])
    din('ssd_logA', [1, 8]); din('sd_pos_d', [128, 1]); din('sd_ldif_d', [128, 64])
    din('sd_mask_d', [128, 64]); din('sd_lp1_d', [16, 64]); din('sd_sel8_d', [8, 2048])
    din('sd_cw_d', [128, 17 * 4]); din('sd_cb_d', [128, 18])
    din('ssd_coef', [16, 8]); din('ssd_nw', [128, 2048]); din('ssd_dbc', [128, 8])
    for nm in ('l1a_wq', 'l1a_wk', 'l1a_wv', 'l1a_wo'):
        din(nm, [DM, DM])
    din('l1a_biases', [128, 24]); din('l1a_alphas', [1, 3]); din('l1a_bo', [128, 8])
    for nm in ('WrT', 'WkT', 'WvT', 'l1b_Wq', 'l1b_Wk', 'l1b_Wv'):
        din(nm, [DM, DM])
    din('l1b_biases', [128, 24])
    RXC = (16 * LPAD) // 128
    din('l1b_relx', [128, RXC]); din('l1b_relcm', [128, RXC]); din('l1b_rmask', [128, RXC])
    dint('l0a_Kfm_loc', [512, T], BF16); dint('l0a_Kfm', [4 * 512, T], BF16)
    dint('l0a_Vtm_loc', [T, 512], BF16); dint('l0a_Vtm', [SEQ, 512], BF16)
    dint('l0a_nrm_loc', [1, 32], F32); dint('l0a_nrm_in', [1, 128], F32)
    dint('ssd_xh', [T, 2048], BF16); dint('ssd_yd', [T, 2048], BF16)
    dint('ssd_zfm', [2048, T], BF16); dint('ssd_cfm', [16, T], BF16)
    dint('ssd_sst', [32, 16, 2048], F32); dint('ssd_G', [16, 2048], F32)
    dint('ssd_Gin', [64, 2048], F32)
    dint('l1a_qloc', [128, 8 * T], BF16); dint('l1a_Kfm_loc', [DM, T], BF16)
    dint('l1a_Kfm', [4 * DM, T], BF16); dint('l1a_Vtm_loc', [T, DM], BF16)
    dint('l1a_Vtm', [SEQ, DM], BF16); dint('l1a_cn', [128, 256], F32)
    dint('l1b_Kfm_loc', [DM, T], BF16); dint('l1b_Kfm', [4 * DM, T], BF16)
    dint('l1b_Vtm_loc', [T, DM], BF16); dint('l1b_Vtm', [SEQ, DM], BF16)
    dint('l1b_nrm_loc', [1, 32], F32); dint('l1b_nrm_in', [1, 128], F32)
    dint('halo_in', [DM, HALO], F32); dint('halo_out', [4 * DM, HALO], F32)

    with contextlib.ExitStack() as stk:
        kb = KB(nc, stk)
        kb.dram = dram
        kb.fused = True
        kb.pw_cols = 32 * 128
        kb.pw_single = True
        dram['ssd_yd_in'] = dram['ssd_yd']; dram['ssd_xh_in'] = dram['ssd_xh']
        dram['ssd_zfm_in'] = dram['ssd_zfm']; dram['ssd_cfm_in'] = dram['ssd_cfm']
        dram['ssd_sst_in'] = dram['ssd_sst']
        dram['l0a_qn'] = dram['l0a_nrm_loc'][0:1, 0:16]
        dram['l1b_qn'] = dram['l1b_nrm_loc'][0:1, 0:16]

        x = kb.sbuf('x_sb', [128, DCH * TH], F32)
        kb.psum('ps0', [128, 2048]); kb.psum('ps1', [128, 2048])
        kb.sbuf('scr_f32', [128, 1024], F32)
        oc = kb.sbuf('ones_col_sb', [128, 1], F32); kb.bufs['ones_col'] = oc
        orow = kb.sbuf('ones_row_sb', [1, 128], F32); kb.bufs['ones_row'] = orow
        ident = kb.sbuf('ident_sb', [128, 128], BF16)
        msel = kb.sbuf('msel_sb', [128, 4], F32)
        kb.dma('sync', oc[:, :], dram['ones_col'])
        kb.dma('sync', orow[:, :], dram['ones_row'])
        kb.dma('sync', msel[:, :], dram['msel'])
        kb.dma('gpsimd', ident[:, :], dram['ident'], inc='ldg')
        kb.dma('sync', x.ap().rearrange("p (kc t) -> p kc t", kc=DCH),
               fm_in(dram['x']))
        ldx = kb.val('ld')
        for e in ('tensor', 'vector', 'scalar'):
            kb.barrier(e, [('ld', ldx), ('ldg', kb.val('ldg'))])
        ocb = kb.sbuf('ones_col_bf', [128, 1], BF16)
        kb.step('vector', lambda v, o=ocb[:, :], i=oc[:, :]: v.tensor_copy(o, i), inc='dv')
        orow32 = kb.sbuf('ones_row32', [33, 128], F32)
        kb.step('vector', lambda v, o=orow32[0:33, :]: v.memset(o, 1.0), inc='dv')

        def LN(kb2, xln, stats):
            emit_ln(kb2, x, xln, stats)

        def scoped_ln(small=False):
            xln = kb.sbuf('xln_sb', [128, DCH * TH], BF16)
            stats = kb.sbuf('stats_sb', [128, TH if small else 2 * TH], F32)
            emit_ln(kb, x, xln, stats)
            return xln, stats

        if nphase >= 1:
         with kb.scope():
            kb.sbuf('a0_q', [64, 8 * T], BF16)
            with kb.scope():
                xln, stats = scoped_ln()
                emit_l0a_qkv(kb, x, xln, stats)
                kb.fence_all()
            ag1 = emit_ag(kb, dram['l0a_Kfm_loc'], dram['l0a_Kfm'], kb.now('st', 'st4'))
            emit_ag(kb, dram['l0a_Vtm_loc'], dram['l0a_Vtm'],
                    [('st_tm%d' % i, kb.val('st_tm%d' % i)) for i in range(4)])
            emit_ag(kb, dram['l0a_nrm_loc'],
                    dram['l0a_nrm_in'].rearrange("one (g c) -> (one g) c", g=4), kb.now('st'))
            kb.barrier('sync', kb.now('cc'))
            kb.barrier('gpsimd', kb.now('cc'))
            with kb.scope():
                emit_l0a_attn(kb, x, None, None)
                kb.fence_all()

        if nphase >= 2:
         with kb.scope():
            fsc = kb.sbuf('fsc_sb', [128, 136], F32)
            kb.dma('sync', fsc[:, :], dram['fsc'], inc='ldf')
            kb.barrier('scalar', [('ldf', kb.val('ldf'))])
            kb.barrier('vector', [('ldf', kb.val('ldf'))])
            xln, stats = scoped_ln(small=True)
            emit_ffn0(kb, x, xln, 'w1T', 'w2T', 'h1T', 'h2T', 'w3T')
            kb.fence_all()

        if nphase >= 3:
         with kb.scope():
            emit_halo_exchange(kb, x)
            for nm, shp, dt_, dk in (('sd_pos', [128, 1], F32, 'sd_pos_d'),
                                     ('sd_ldif', [128, 64], BF16, 'sd_ldif_d'),
                                     ('sd_mask', [128, 64], BF16, 'sd_mask_d'),
                                     ('sd_lp1', [16, 64], BF16, 'sd_lp1_d'),
                                     ('sd_cw', [128, 17 * 4], F32, 'sd_cw_d'),
                                     ('sd_cb', [128, 18], F32, 'sd_cb_d')):
                b = kb.sbuf(nm, shp, dt_)
                if dt_ == BF16:
                    kb.dma('gpsimd', b[0:shp[0], :], dram[dk], inc='ldg3')
                else:
                    kb.dma('sync', b[0:shp[0], :], dram[dk], inc='ld3b')
            kb.fence_all()
            with kb.scope():
                xln, stats = scoped_ln()
                emit_ssd_a(kb, x, xln, stats)
                kb.fence_all()
            emit_ag(kb, dram['ssd_G'], dram['ssd_Gin'], kb.now('st'))
            kb.barrier('sync', kb.now('cc'))
            with kb.scope():
                emit_ssd_b(kb, x, None, None)
                kb.fence_all()

        if nphase >= 4:
         with kb.scope():
            fsc = kb.sbuf('fsc_sb', [128, 136], F32)
            kb.dma('sync', fsc[:, :], dram['fsc'], inc='ldf')
            kb.barrier('scalar', [('ldf', kb.val('ldf'))])
            kb.barrier('vector', [('ldf', kb.val('ldf'))])
            xln, stats = scoped_ln(small=True)
            emit_ffn0(kb, x, xln, 'w1T', 'w2T', 'h1T', 'h2T', 'w3T')
            kb.fence_all()

        if nphase >= 5:
         with kb.scope():
            with kb.scope():
                kb.sbuf('y1_bias', [128, 24], F32)
                kb.dma('sync', kb.bufs['y1_bias'][:, :], dram['l1a_biases'], inc='ld5b')
                kb.sbuf('y1_alpha', [1, 3], F32)
                kb.dma('sync', kb.bufs['y1_alpha'][0:1, :], dram['l1a_alphas'], inc='ld5b')
                kb.fence_all()
                xln, stats = scoped_ln()
                emit_l1a_qkv(kb, x, xln, stats)
                kb.fence_all()
            emit_ag(kb, dram['l1a_Kfm_loc'], dram['l1a_Kfm'], kb.now('stk', 'st'))
            emit_ag(kb, dram['l1a_Vtm_loc'], dram['l1a_Vtm'],
                    [('st_tm%d' % i, kb.val('st_tm%d' % i)) for i in range(4)])
            kb.barrier('sync', kb.now('cc'))
            kb.barrier('gpsimd', kb.now('cc'))
            with kb.scope():
                kb.sbuf('y1_bo', [128, 8], F32)
                kb.dma('sync', kb.bufs['y1_bo'][:, :], dram['l1a_bo'], inc='ld5c')
                kb.fence_all()
                emit_l1a_attn(kb, x, None, None)
                kb.fence_all()

        if nphase >= 6:
         with kb.scope():
            emit_halo_exchange(kb, x)
            mu = kb.sbuf('mu_sb', [128, 32], F32)
            kb.dma('sync', mu[:, 0:16], dram['mu_rk'], inc='ld6b')
            kb.fence_all()
            xln, stats = scoped_ln()
            emit_l1ffn(kb, x, xln, 'WrT', 'WkT', 'WvT')
            kb.fence_all()

        if nphase >= 7:
         with kb.scope():
            kb.sbuf('b1_q', [128, 8 * T], BF16)
            with kb.scope():
                kb.sbuf('b1_bias', [128, 24], F32)
                kb.dma('sync', kb.bufs['b1_bias'][:, :], dram['l1b_biases'], inc='ld7b')
                kb.fence_all()
                xln, stats = scoped_ln()
                emit_l1b_qkv(kb, x, xln, stats)
                kb.fence_all()
            emit_ag(kb, dram['l1b_Kfm_loc'], dram['l1b_Kfm'], kb.now('stk', 'st'))
            emit_ag(kb, dram['l1b_Vtm_loc'], dram['l1b_Vtm'],
                    [('st_tm%d' % i, kb.val('st_tm%d' % i)) for i in range(4)])
            emit_ag(kb, dram['l1b_nrm_loc'],
                    dram['l1b_nrm_in'].rearrange("one (g c) -> (one g) c", g=4), kb.now('st'))
            kb.barrier('sync', kb.now('cc'))
            kb.barrier('gpsimd', kb.now('cc'))
            with kb.scope():
                kb.bufs['b1_qv'] = kb.bufs['b1_q']
                kb.sbuf('j128_sb', [128, 128], BF16)
                kb.dma('gpsimd', kb.bufs['j128_sb'][:, :], dram['j128'], inc='ldg7b')
                kb.fence_all()
                emit_l1b_attn(kb, x, None, None)
                kb.fence_all()
            kb.bufs.pop('b1_qv', None)

        if nphase >= 8:
         with kb.scope():
            emit_halo_exchange(kb, x)
            mu = kb.sbuf('mu_sb', [128, 32], F32)
            kb.dma('sync', mu[:, 0:16], dram['mu_rk'], inc='ld8b')
            kb.fence_all()
            xln, stats = scoped_ln()
            emit_l1ffn(kb, x, xln, 'WrT', 'WkT', 'WvT')
            kb.fence_all()

        out3 = dram['x_out'].rearrange("(kc p) t -> p kc t", p=128)
        in3 = x.ap().rearrange("p (kc th) -> p kc th", kc=DCH)[:, :, HALO:HALO + T]
        kb.step('sync', lambda e: e.dma_start(out=out3, in_=in3),
                waits=kb.now('dv', 'ac', 'pe'), inc='stx', inc_n=16)
        kb.step('sync', lambda e: e.nop(), waits=kb.now('stx'))

        with nc.Block() as block:
            kb.finalize(block)
    return nc


def _fused_maps(d):
    xsh = _shard_x_arrays(d['L0_attn_a_x'].astype(np.float32))
    lamvec = np.concatenate([d['L0_attn_a_lambda_q1'], d['L0_attn_a_lambda_k1'],
                             d['L0_attn_a_lambda_q2'], d['L0_attn_a_lambda_k2']]
                            )[None, :].astype(np.float32)
    gT = np.ascontiguousarray(np.asarray(d['L0_attn_a_gamma'], np.float32).reshape(64, 1))
    fsc = np.zeros((128, 136), np.float32)
    for i, k in enumerate(['w1_s', 'w2_s', 'h1_s', 'h2_s']):
        fsc[:, i * 32:(i + 1) * 32] = np.asarray(d['L0_ffn_' + k], np.float32).reshape(32, 128).T
    fsc[:, 128:136] = np.asarray(d['L0_ffn_w3_s'], np.float32).reshape(8, 128).T
    p_ = np.arange(128)
    lidx = np.arange(64)
    pos = (63 - (p_ % 64)).astype(np.float32)[:, None]
    ldif = (lidx[None, :] - (p_ % 64)[:, None]).astype(np.float32)
    ldif[lidx[None, :] < (p_ % 64)[:, None].repeat(64, 1)] = 30000.0
    maskc = (lidx[None, :] >= (p_ % 64)[:, None]).astype(np.float32)
    lp1 = np.broadcast_to((lidx + 1).astype(np.float32)[None, :], (16, 64)).copy()
    cw = np.zeros((128, 17 * 4), np.float32)
    cbv = np.zeros((128, 18), np.float32)
    for dd in range(17):
        nch = 128 if dd < 16 else 32
        cw[0:nch, dd * 4:(dd + 1) * 4] = d['L0_attn_b_conv_w'][dd * 128:dd * 128 + nch, 0, :]
        cbv[0:nch, dd] = d['L0_attn_b_conv_b'][dd * 128:dd * 128 + nch]
    cbv[32:40, 17] = d['L0_attn_b_dt_bias']
    sel8 = np.zeros((8, 2048), np.float32)
    for hh in range(8):
        sel8[hh, hh * 256:(hh + 1) * 256] = 1.0
    bias_pack = np.concatenate([_chunk_major(d['L1_attn_a_bq'], 8),
                                _chunk_major(d['L1_attn_a_bk'], 8),
                                _chunk_major(d['L1_attn_a_bv'], 8)], 1)
    alphas = np.concatenate([d['L1_attn_a_aq'], d['L1_attn_a_ak'],
                             d['L1_attn_a_av']])[None, :].astype(np.float32)
    bo_pack = _chunk_major(d['L1_attn_a_bo'], 8)
    mu_rk = np.ascontiguousarray(np.concatenate([_chunk_major(d['L1_ffn_mu_r'], 8),
                                                 _chunk_major(d['L1_ffn_mu_k'], 8)], 1))
    b1bias = np.concatenate([_chunk_major(d['L1_attn_b_bq'], 8),
                             _chunk_major(d['L1_attn_b_bk'], 8),
                             _chunk_major(d['L1_attn_b_bv'], 8)], 1)
    re_ = np.asarray(d['L1_attn_b_rel_emb'], np.float32)
    RXC = (16 * LPAD) // 128

    def relx_for(q0):
        u = np.arange(LPAD)
        dd = u + U0 - q0
        valid = (dd > 4096) & (dd <= 8191)
        idx = np.clip(8191 - dd, 0, 4094)
        rx = np.where(valid[None, :], re_[:, idx], 0.0).astype(np.float32)
        mk = np.broadcast_to(valid[None, :], (16, LPAD)).astype(np.float32).copy()
        cm = np.broadcast_to(re_[:, 4094:4095], (16, LPAD)).astype(np.float32).copy()
        rs = lambda a: np.ascontiguousarray(a.reshape(128, RXC))
        return rs(rx), rs(mk), rs(cm)

    maps = []
    for c in range(NCORES):
        b, q = c // 4, c % 4
        coef = np.zeros((16, 8), np.float32)
        for j in range(4):
            coef[:, j] = 32.0 * max(q - 1 - j, 0)
            coef[:, 4 + j] = 1.0 if j < q else 0.0
        mselv = np.zeros((128, 4), np.float32)
        if q > 0:
            mselv[:, q - 1] = 1.0
        rx, mk, cm = relx_for(q * T)
        maps.append(dict(
            x=xsh[c], ones_col=ONES_COL, ones_row=ONES_ROW, ident=IDENT_NP, msel=mselv,
            mu_rk=mu_rk, fsc=fsc, j128=J128_NP,
            l0a_WqT=d['L0_attn_a_Wq'], l0a_Wkv=d['L0_attn_a_Wkv'],
            l0a_WoutT=d['L0_attn_a_Wout'], l0a_lam=lamvec, l0a_gammaT=gT,
            w1T=np.ascontiguousarray(d['L0_ffn_w1_w'].T),
            w2T=np.ascontiguousarray(d['L0_ffn_w2_w'].T),
            h1T=np.ascontiguousarray(d['L0_ffn_h1_w'].T),
            h2T=np.ascontiguousarray(d['L0_ffn_h2_w'].T),
            w3T=np.ascontiguousarray(d['L0_ffn_w3_w'].T),
            ssd_WinT=np.ascontiguousarray(d['L0_attn_b_W_in'].T),
            ssd_WoutT=np.ascontiguousarray(d['L0_attn_b_W_out'].T),
            ssd_logA=np.asarray(d['L0_attn_b_log_A'], np.float32)[None, :],
            sd_pos_d=pos, sd_ldif_d=ldif, sd_mask_d=maskc, sd_lp1_d=lp1, sd_sel8_d=sel8,
            sd_cw_d=cw, sd_cb_d=cbv, ssd_coef=coef,
            ssd_nw=np.broadcast_to(np.asarray(d['L0_attn_b_norm_w'], np.float32), (128, 2048)).copy(),
            ssd_dbc=np.broadcast_to(np.asarray(d['L0_attn_b_D_skip'], np.float32), (128, 8)).copy(),
            l1a_wq=d['L1_attn_a_wq'], l1a_wk=d['L1_attn_a_wk'], l1a_wv=d['L1_attn_a_wv'],
            l1a_wo=d['L1_attn_a_wo'], l1a_biases=bias_pack, l1a_alphas=alphas, l1a_bo=bo_pack,
            WrT=np.ascontiguousarray(d['L1_ffn_W_r'].T),
            WkT=np.ascontiguousarray(d['L1_ffn_W_k'].T),
            WvT=np.ascontiguousarray(d['L1_ffn_W_v'].T),
            l1b_Wq=d['L1_attn_b_Wq'], l1b_Wk=d['L1_attn_b_Wk'], l1b_Wv=d['L1_attn_b_Wv'],
            l1b_biases=b1bias, l1b_relx=rx, l1b_relcm=cm, l1b_rmask=mk))
    return maps


def kernel_fused(nphase=8, **d):
    from concourse.bass_utils import run_bass_kernel_spmd
    nc = _prog('fused%d' % nphase, lambda: build_fused(nphase))
    maps = _fused_maps(d)
    res = run_bass_kernel_spmd(nc, maps, core_ids=list(range(NCORES)))
    xfull = np.zeros((2, SEQ, DM), np.float32)
    for c in range(NCORES):
        b, q = c // 4, c % 4
        xfull[b, q * T:(q + 1) * T] = np.asarray(res.results[c]['x_out']).T
    return xfull


def kernel(**inp):
    return kernel_fused(**inp)
